# version 04
# baseline (speedup 1.0000x reference)
import numpy as np
import ml_dtypes
import concourse.bass as bass
import concourse.mybir as mybir
from concourse.bass_utils import run_bass_kernel_spmd

F32 = mybir.dt.float32
BF16 = mybir.dt.bfloat16
I32 = mybir.dt.int32
BF = ml_dtypes.bfloat16
AF = mybir.ActivationFunctionType
OP = mybir.AluOpType
AX = mybir.AxisListType

L, D, H, DH, T, NT, C = 4096, 1024, 4, 256, 512, 8, 128
NCH = 8
NBLK = 4
HID = 3072
RH = 2176
RK = 9
NJG = HID // 128
FFN_SC = 1.0 / (127.0 * 127.0)

DEBUG_TAPS = []
STAGE = 99
DEBUG_RESULTS = None
RISKY_DELAY = 500
QKV_TENSORS = 3
QKV_NX = 8
QKV_CONV = True


class Em:
    def __init__(self, nc):
        self.nc = nc
        self.e = {'T': nc.tensor, 'V': nc.vector, 'A': nc.scalar, 'P': nc.gpsimd, 'S': nc.sync}
        self.sem = {x: nc.alloc_semaphore(f"sem{x}") for x in 'TVAPS'}
        self.NDL = 8
        self.semd = {x: [nc.alloc_semaphore(f"semd{x}{i}") for i in range(self.NDL)]
                     for x in 'SP'}
        self.cnt = {x: 0 for x in 'TVAPS'}
        self.cntd = {x: 0 for x in 'SP'}
        self.cntdl = {(x, i): 0 for x in 'SP' for i in range(self.NDL)}
        self.water = {}
        self.state = {}

    def _wait(self, eng, key, val, risky=False):
        w = self.water.get((eng, key), 0)
        if w >= val:
            return
        if key[0] == 'd':
            sem = self.semd[key[1]][int(key[2:])]
        else:
            sem = self.sem[key]
        self.e[eng].wait_ge(sem, val)
        self.water[(eng, key)] = val
        if risky and RISKY_DELAY:
            self.e[eng].nop(cycle_cnt=RISKY_DELAY)

    def op(self, eng, method, *args, reads=(), writes=(), dma=False, risky=False, **kw):
        raw = {}
        war = {}
        for b in reads:
            st = self.state.get(b)
            if st is not None and st[0] is not None:
                k, v, rk = st[0]
                if raw.get(k, (0, False))[0] < v:
                    raw[k] = (v, rk)
        for b in writes:
            st = self.state.get(b)
            if st is not None:
                if st[0] is not None:
                    k, v, rk = st[0]
                    if war.get(k, 0) < v:
                        war[k] = v
                for k, v in st[1].items():
                    if war.get(k, 0) < v:
                        war[k] = v
        if dma:
            lane = self.cntd[eng] % self.NDL
            mykey = f"d{eng}{lane}"
        else:
            mykey = eng
        for k, (v, rk) in raw.items():
            if dma and k.startswith('d' + eng):
                continue
            self._wait(eng, k, v, risky=rk and k != eng)
        for k, v in war.items():
            if (dma and k.startswith('d' + eng)) or k == mykey or k == eng:
                continue
            self._wait(eng, k, v)
        if dma and self.cntdl[(eng, lane)] >= 1:
            self._wait(eng, mykey, 16 * self.cntdl[(eng, lane)])
        inst = getattr(self.e[eng], method)(*args, **kw)
        if dma:
            self.cntd[eng] += 1
            self.cntdl[(eng, lane)] += 1
            inst.then_inc(self.semd[eng][lane], 16)
            stamp = (mykey, 16 * self.cntdl[(eng, lane)], False)
        else:
            self.cnt[eng] += 1
            inst.then_inc(self.sem[eng], 1)
            stamp = (eng, self.cnt[eng], risky)
        for b in reads:
            st = self.state.get(b)
            if st is None:
                st = (None, {})
                self.state[b] = st
            st[1][stamp[0]] = stamp[1]
        for b in writes:
            self.state[b] = (stamp, {})
        return inst

    def mm(self, out, lhsT, rhs, *, reads, writes, start=True, stop=True, tile_position=None):
        try:
            nfree = int(out.free_size() if callable(out.free_size) else out.free_size)
        except Exception:
            nfree = 512
        return self.op('T', 'matmul', out, lhsT, rhs, start=start, stop=stop,
                       tile_position=tile_position, reads=reads, writes=writes,
                       risky=(nfree < 256))

    def trans(self, out, in_, ident, *, reads, writes):
        return self.op('T', 'transpose', out, in_, ident, reads=reads, writes=writes,
                       risky=True)

    def tt(self, eng, out, a, b, opc, *, reads, writes):
        return self.op(eng, 'tensor_tensor', out=out, in0=a, in1=b, op=opc,
                       reads=reads, writes=writes)

    def ts(self, eng, out, a, s1, op0, s2=None, op1=None, *, reads, writes):
        kw = dict(out=out, in0=a, scalar1=s1, scalar2=s2, op0=op0)
        if op1 is not None:
            kw['op1'] = op1
        return self.op(eng, 'tensor_scalar', reads=reads, writes=writes, **kw)

    def stt(self, eng, out, a, s, b, op0, op1, *, reads, writes):
        return self.op(eng, 'scalar_tensor_tensor', out=out, in0=a, scalar=s, in1=b,
                       op0=op0, op1=op1, reads=reads, writes=writes)

    def act(self, out, in_, func, *, reads, writes, bias=None, scale=None):
        kw = {}
        if bias is not None:
            kw['bias'] = bias
        if scale is not None:
            kw['scale'] = scale
        return self.op('A', 'activation', out, in_, func, reads=reads, writes=writes, **kw)

    def copy(self, eng, out, in_, *, reads, writes):
        if eng == 'A':
            return self.op('A', 'copy', out, in_, reads=reads, writes=writes)
        return self.op(eng, 'tensor_copy', out, in_, reads=reads, writes=writes)

    def dma(self, q, out, in_, *, reads, writes):
        return self.op(q, 'dma_start', out=out, in_=in_, dma=True, reads=reads, writes=writes)

    def fence(self, eng, keys):
        inst = self.e[eng].drain(fusable=False)
        self.cnt[eng] += 1
        inst.then_inc(self.sem[eng], 1)
        stamp = (eng, self.cnt[eng], False)
        for b in keys:
            st = self.state.get(b)
            if st is not None:
                self.state[b] = (stamp, dict(st[1]))
            else:
                self.state[b] = (stamp, {})


def build_program(taps=(), stage=99):
    nc = bass.Bass("TRN2", target_bir_lowering=False)
    em = Em(nc)
    dbg = {}

    def din(name, shape, dt):
        return nc.dram_tensor(name, shape, dt, kind="ExternalInput").ap()

    x_h = din("x", [D, L], F32)
    wq_h = din("wq", [128, 8 * D], F32)
    wk_h = din("wk", [128, 8 * D], F32)
    wv_h = din("wv", [128, 8 * D], F32)
    wb_h = din("wb", [128, 32], F32)
    wo_h = din("wo", [128, 8 * D], F32)
    rw1_h = din("rw1", [128, RK * RH], F32)
    rb1_h = din("rb1", [128, 17], F32)
    rw2_h = din("rw2", [128, 17 * 16], F32)
    rb2_h = din("rb2", [128, 1], F32)
    cq_h = din("cq", [128, 32], F32)
    ck_h = din("ck", [128, 32], F32)
    cv_h = din("cv", [128, 32], F32)
    cl_h = din("cl", [128, 56], F32)
    cm_h = din("cm", [128, 200], F32)
    mw_h = din("mw", [128, 8], F32)
    onorm_h = din("onorm", [128, 8], F32)
    wg_h = din("wg", [128, 8 * HID], I32)
    wu_h = din("wu", [128, 8 * HID], I32)
    wd_h = din("wd", [128, NJG * D], I32)
    id_h = din("identbf", [128, 128], BF16)
    idf_h = din("identf32", [128, 128], F32)
    ones_h = din("onesbf", [128, 128], BF16)
    onesf_h = din("onesf32", [128, 1], F32)
    mlow_h = din("masklow", [128, 128], BF16)
    mupi_h = din("maskupincl", [128, 128], BF16)

    out_h = nc.dram_tensor("out", [D, L], F32, kind="ExternalOutput").ap()

    wg_b = nc.dram_tensor("wg_bf", [128, NJG * 1024], BF16, kind="Internal").ap()
    wu_b = nc.dram_tensor("wu_bf", [128, NJG * 1024], BF16, kind="Internal").ap()
    wd_b = nc.dram_tensor("wd_bf", [128, NJG * 1024], BF16, kind="Internal").ap()
    rw1_b = nc.dram_tensor("rw1_bf", [128, 17 * RK * 128], BF16, kind="Internal").ap()

    def sb(name, shape, dt):
        return nc.alloc_sbuf_tensor(name, shape, dt)

    wq = sb("wq_s", [128, 8 * D], BF16)
    wk = sb("wk_s", [128, 8 * D], BF16)
    wv = sb("wv_s", [128, 8 * D], BF16)
    wb = sb("wb_s", [128, 32], BF16)
    wo = sb("wo_s", [128, 8 * D], BF16)
    rw2 = sb("rw2_s", [128, 17 * 16], BF16)
    rb1 = sb("rb1_s", [128, 17], F32)
    rb2 = sb("rb2_s", [128, 1], F32)
    cq = sb("cq_s", [128, 32], F32)
    ck = sb("ck_s", [128, 32], F32)
    cv = sb("cv_s", [128, 32], F32)
    cl = sb("cl_s", [128, 56], F32)
    cm = sb("cm_s", [128, 200], F32)
    mw = sb("mw_s", [128, 8], F32)
    onorm = sb("onorm_s", [128, 8], F32)
    w2 = sb("w2_s", [128, 8], F32)
    mw2 = sb("mw2_s", [128, 8], F32)
    idb = sb("idb_s", [128, 128], BF16)
    idf = sb("idf_s", [128, 128], F32)
    onesb = sb("onesb_s", [128, 128], BF16)
    onesf = sb("onesf_s", [128, 1], F32)
    mlow = sb("mlow_s", [128, 128], BF16)
    mupi = sb("mupi_s", [128, 128], BF16)

    x_s = sb("x_s", [128, NCH * T], F32)
    xn = sb("xn_s", [128, NCH * T], BF16)
    q_s = sb("q_s", [128, NCH * T], BF16)
    k_s = sb("k_s", [128, NCH * T], BF16)
    v_s = sb("v_s", [128, NCH * (24 + T)], BF16)
    do_s = sb("do_s", [128, NCH * T], BF16)
    lo_s = sb("lo_s", [128, NCH * T], BF16)
    mi_s = sb("mi_s", [128, NCH * T], BF16)
    pre = sb("pre_s", [128, 520], BF16)
    pre2 = sb("pre2_s", [128, 520], BF16)
    carry = sb("carry_s", [128, 3 * NCH * 3], BF16)
    sq = sb("sq_s", [128, T], BF16)
    sq2 = sb("sq2_s", [128, T], BF16)
    mxt = sb("mxt_s", [128, T], BF16)
    tok = sb("tok_s", [128, 64], F32)
    tokb = sb("tokb_s", [128, 64], F32)
    toke = sb("toke_s", [128, 64], F32)
    ftmp = sb("ftmp_s", [128, 64], F32)
    mbc = sb("mbc_s", [128, T], BF16)
    rbc = sb("rbc_s", [128, T], BF16)
    fbc = sb("fbc_s", [128, T], BF16)
    diag = sb("diag_s", [128, 128], BF16)
    btok = sb("btok_s", [128, 16], F32)
    lgt = sb("lgt_s", [16, T], F32)
    feats = sb("feats_s", [128, T], BF16)
    h1a = sb("h1a_s", [128, T], BF16)
    h1b = sb("h1b_s", [128, T], BF16)
    rw1s = [sb(f"rw1s{i}", [128, RK * 128], BF16) for i in range(2)]
    wgs = [sb(f"wgs{i}", [128, 1024], BF16) for i in range(2)]
    wus = [sb(f"wus{i}", [128, 1024], BF16) for i in range(2)]
    wds = [sb(f"wds{i}", [128, 1024], BF16) for i in range(2)]
    sg = sb("sg_s", [128, T], BF16)
    hf = sb("hf_s", [128, T], BF16)
    bigp = sb("bigp_s", [128, 1024], BF16)

    dA = [sb(f"dA{h}", [128, 128], BF16) for h in range(H)]
    dAT = [sb(f"dAT{h}", [128, 128], BF16) for h in range(H)]
    dM = [sb(f"dM{h}", [128, 128], BF16) for h in range(H)]
    dMT = [sb(f"dMT{h}", [128, 128], BF16) for h in range(H)]
    dM2 = [sb(f"dM2{h}", [128, 128], BF16) for h in range(H)]
    dMT2 = [sb(f"dMT2{h}", [128, 128], BF16) for h in range(H)]
    dRa = [sb(f"dRa{h}", [128, 128], BF16) for h in range(H)]
    dRb = [sb(f"dRb{h}", [128, 128], BF16) for h in range(H)]
    dAttnT = [sb(f"dAttnT{h}", [128, 128], BF16) for h in range(H)]
    dKtm = [sb(f"dKtm{h}", [128, 256], BF16) for h in range(H)]
    dKbtm = [sb(f"dKbtm{h}", [128, 256], BF16) for h in range(H)]
    dVbtm = [sb(f"dVbtm{h}", [128, 256], BF16) for h in range(H)]
    dU = [sb(f"dU{h}", [128, 256], BF16) for h in range(H)]
    dW = [sb(f"dW{h}", [128, 256], BF16) for h in range(H)]
    dSbf = [sb(f"dSbf{h}", [128, 512], BF16) for h in range(H)]

    Sps = [nc.alloc_psum_tensor(f"Sps{h}", [128, 512], F32) for h in range(H)]
    Pps = [nc.alloc_psum_tensor(f"Pps{i}", [128, 512], F32) for i in range(4)]
    Pbf = [p.ap().bitcast(BF16) for p in Pps]

    SB = [f"SB{h}" for h in range(H)]
    PB = [f"PB{i}" for i in range(4)]

    XS = [f"x{c}" for c in range(NCH)]
    XN = [f"xn{c}" for c in range(NCH)]
    QS = [f"q{c}" for c in range(NCH)]
    KS = [f"k{c}" for c in range(NCH)]
    VS = [f"v{c}" for c in range(NCH)]
    DOS = [f"do{c}" for c in range(NCH)]
    LOS = [f"lo{c}" for c in range(NCH)]
    MIS = [f"mi{c}" for c in range(NCH)]

    def vcol(c, t):
        return c * (24 + T) + 24 + t

    def tap(name, region, reads):
        if name not in taps:
            return
        shp = list(region.shape)
        dt = region.dtype
        o = nc.dram_tensor("dbg_" + name, shp, dt, kind="ExternalOutput").ap()
        em.dma('S', o, region, reads=tuple(reads), writes=())
        dbg[name] = (dt == BF16)

    for dst, src in [(idb, id_h), (idf, idf_h), (onesb, ones_h), (onesf, onesf_h),
                     (mlow, mlow_h), (mupi, mupi_h), (rb1, rb1_h), (rb2, rb2_h),
                     (cq, cq_h), (ck, ck_h), (cv, cv_h), (cl, cl_h), (cm, cm_h),
                     (mw, mw_h), (onorm, onorm_h)]:
        em.dma('S', dst[:], src, reads=(), writes=(dst.name,))
    for dst, src in [(wq, wq_h), (wk, wk_h), (wv, wv_h), (wb, wb_h), (wo, wo_h), (rw2, rw2_h)]:
        em.dma('P', dst[:], src, reads=(), writes=(dst.name,))
    for (srch, dstd, key) in [(wg_h, wg_b, "wg_b"), (wu_h, wu_b, "wu_b")]:
        for kk in range(8):
            for pp in range(3):
                em.dma('P', bigp[:, :1024], srch[:, kk * HID + pp * 1024:kk * HID + (pp + 1) * 1024],
                       reads=("bigp",), writes=("bigp",))
                dst3 = dstd.rearrange("p (j x) -> p j x", x=1024)[:, pp * 8:(pp + 1) * 8,
                                                                 kk * 128:(kk + 1) * 128]
                src3 = bigp[:, :1024].rearrange("p (j c) -> p j c", c=128)
                em.dma('S', dst3, src3, reads=("bigp",), writes=(key,))
    for jj in range(NJG):
        em.dma('P', bigp[:, :1024], wd_h[:, jj * 1024:(jj + 1) * 1024],
               reads=("bigp",), writes=("bigp",))
        em.dma('S', wd_b[:, jj * 1024:(jj + 1) * 1024], bigp[:, :1024],
               reads=("bigp",), writes=("wd_b",))
    for kk in range(RK):
        for pp in range(4):
            w = 512 if pp < 3 else 640
            o = pp * 512
            nsl = w // 128
            em.dma('P', bigp[:, :w], rw1_h[:, kk * RH + o:kk * RH + o + w],
                   reads=("bigp",), writes=("bigp",))
            dst3 = rw1_b.rearrange("p (n x) -> p n x", x=RK * 128)[:, pp * 4:pp * 4 + nsl,
                                                                   kk * 128:(kk + 1) * 128]
            src3 = bigp[:, :w].rearrange("p (n j) -> p n j", j=128)
            em.dma('S', dst3, src3, reads=("bigp",), writes=("rw1_b",))

    em.tt('V', w2[:], mw[:], onorm[:], OP.mult, reads=(mw.name, onorm.name), writes=(w2.name,))
    em.tt('V', mw2[:], mw[:], mw[:], OP.mult, reads=(mw.name,), writes=(mw2.name,))
    em.op('P', 'memset', feats[:], 0.0, reads=(), writes=("feats",))
    em.op('P', 'memset', carry[:], 0.0, reads=(), writes=("carry",))
    em.op('P', 'memset', v_s[:], 0.0, reads=(), writes=tuple(VS))

    for tile in range(NT):
        t0 = tile * T

        src3 = x_h.rearrange("(c p) L -> p c L", p=128)[:, :, t0:t0 + T]
        dst3 = x_s[:].rearrange("p (c t) -> p c t", c=NCH)
        em.dma('S', dst3, src3, reads=tuple(XS), writes=tuple(XS))

        for c in range(NCH):
            for b in range(NBLK):
                em.mm(Pps[b][:, 0:1], x_s[:, c * T + b * 128:c * T + (b + 1) * 128],
                      onesf[:], reads=(XS[c], onesf.name), writes=(PB[b],),
                      start=(c == 0), stop=(c == NCH - 1))
        for c in range(NCH):
            em.act(sq[:], x_s[:, c * T:(c + 1) * T], AF.Square, reads=(XS[c],), writes=("sq",))
            for b in range(NBLK):
                em.mm(Pps[b][:, 1:2], sq[:, b * 128:(b + 1) * 128], onesb[:, 0:1],
                      reads=("sq", onesb.name), writes=(PB[b],),
                      start=(c == 0), stop=(c == NCH - 1))
        em.fence('T', PB)
        for b in range(NBLK):
            em.copy('V', tok[:, b:b + 1], Pps[b][:, 0:1], reads=(PB[b],), writes=("tok",))
            em.copy('V', tok[:, 4 + b:5 + b], Pps[b][:, 1:2], reads=(PB[b],), writes=("tok",))
        for b in range(NBLK):
            em.copy('V', tok[:, 32 + b:33 + b], Pps[b][:, 0:1], reads=(PB[b],), writes=("tok",))
            em.copy('V', tok[:, 36 + b:37 + b], Pps[b][:, 1:2], reads=(PB[b],), writes=("tok",))
        tap(f"tokraw_t{tile}", tok[:], ("tok",))
        em.ts('V', tok[:, 0:8], tok[:, 0:8], 1.0 / D, OP.mult, reads=("tok",), writes=("tok",))
        em.tt('V', tokb[:, 0:4], tok[:, 0:4], tok[:, 0:4], OP.mult,
              reads=("tok",), writes=("tokb",))
        em.stt('V', tok[:, 8:12], tok[:, 4:8], 1e-5, tokb[:, 0:4], OP.add, OP.subtract,
               reads=("tok", "tokb"), writes=("tok",))
        em.copy('V', tok[:, 40:44], tok[:, 8:12], reads=("tok",), writes=("tok",))
        em.act(tok[:, 8:12], tok[:, 8:12], AF.Sqrt, reads=("tok",), writes=("tok",))
        em.copy('V', tok[:, 44:48], tok[:, 8:12], reads=("tok",), writes=("tok",))
        em.op('V', 'reciprocal', tok[:, 12:16], tok[:, 8:12], reads=("tok",), writes=("tok",))
        for b in range(NBLK):
            em.ts('V', diag[:], idb[:], tok[:, 12 + b:13 + b], OP.mult,
                  reads=(idb.name, "tok"), writes=("diag",))
            em.mm(Pps[1][:, b * 128:(b + 1) * 128], onesb[:], diag[:],
                  reads=("diag", onesb.name), writes=(PB[1],), start=True, stop=True)
            em.ts('V', diag[:], idb[:], tok[:, b:b + 1], OP.mult,
                  reads=(idb.name, "tok"), writes=("diag",))
            em.mm(Pps[2][:, b * 128:(b + 1) * 128], onesb[:], diag[:],
                  reads=("diag", onesb.name), writes=(PB[2],), start=True, stop=True)
        em.copy('A', rbc[:], Pps[1][:, 0:T], reads=(PB[1],), writes=("rbc",))
        em.copy('A', mbc[:], Pps[2][:, 0:T], reads=(PB[2],), writes=("mbc",))
        tap(f"tok_t{tile}", tok[:], ("tok",))
        tap(f"mbc_t{tile}", mbc[:], ("mbc",))
        tap(f"rbc_t{tile}", rbc[:], ("rbc",))
        for c in range(NCH):
            em.tt('V', mxt[:], x_s[:, c * T:(c + 1) * T], mbc[:], OP.subtract,
                  reads=(XS[c], "mbc"), writes=("mxt",))
            em.tt('V', xn[:, c * T:(c + 1) * T], mxt[:], rbc[:], OP.mult,
                  reads=("mxt", "rbc"), writes=(XN[c],))
        tap(f"xn_t{tile}", xn[:], XN)

        if stage < 1:
            break

        for c in range(NCH):
            em.mm(Pps[3][0:4, 0:T], wb[:, c * 4:(c + 1) * 4], xn[:, c * T:(c + 1) * T],
                  reads=(wb.name, XN[c]), writes=(PB[3],), start=(c == 0), stop=(c == NCH - 1))
        em.copy('V', lgt[0:4, :], Pps[3][0:4, 0:T], reads=(PB[3],), writes=("lgt",))
        for b in range(NBLK):
            em.trans(Pps[0][0:128, b * 4:(b + 1) * 4], lgt[0:4, b * 128:(b + 1) * 128],
                     idf[0:4, 0:4], reads=("lgt", idf.name), writes=(PB[0],))
        em.fence('T', (PB[0],))
        em.act(btok[:], Pps[0][:, 0:16], AF.Tanh, scale=0.5, reads=(PB[0],), writes=("btok",))
        em.ts('V', btok[:], btok[:], 0.5, OP.mult, 0.5, OP.add, reads=("btok",), writes=("btok",))
        tap(f"btok_t{tile}", btok[:], ("btok",))

        if stage < 2:
            break

        for (X, W_, CW, dst, dkeys, isv) in [
                (0, wq, cq, q_s, QS, False), (1, wk, ck, k_s, KS, False),
                (2, wv, cv, v_s, VS, True)][:QKV_TENSORS]:
            for n in range(NCH)[:QKV_NX]:
                pb, pk = Pps[n % 2], PB[n % 2]
                for c in range(NCH):
                    em.mm(pb[:, 0:T], W_[:, c * D + n * 128:c * D + (n + 1) * 128],
                          xn[:, c * T:(c + 1) * T], reads=(W_.name, XN[c]), writes=(pk,),
                          start=(c == 0), stop=(c == NCH - 1))
                ci = X * NCH + n
                em.copy('V', pre[:, 1:4], carry[:, ci * 3:ci * 3 + 3],
                        reads=("carry",), writes=("pre",))
                em.copy('A', pre[:, 4:4 + T], pb[:, 0:T], reads=(pk,), writes=("pre",))
                em.copy('V', carry[:, ci * 3:ci * 3 + 3], pre[:, 1 + T:4 + T],
                        reads=("pre",), writes=("carry",))
                em.copy('V', pre2[:, 0:3 + T], pre[:, 1:4 + T], reads=("pre",), writes=("pre2",))
                if not QKV_CONV:
                    em.act(dst[:, (vcol(n, 0) if isv else n * T):(vcol(n, 0) if isv else n * T) + T],
                           pre[:, 4:4 + T], AF.Silu, reads=("pre",), writes=(dkeys[n],))
                    continue
                em.ts('V', sq[:], pre2[:, 0:T], CW[:, n * 4:n * 4 + 1], OP.mult,
                      reads=("pre2", CW.name), writes=("sq",))
                em.stt('V', sq[:], pre[:, 2:2 + T], CW[:, n * 4 + 1:n * 4 + 2], sq[:],
                       OP.mult, OP.add, reads=("pre", CW.name, "sq"), writes=("sq",))
                em.stt('V', sq[:], pre2[:, 2:2 + T], CW[:, n * 4 + 2:n * 4 + 3], sq[:],
                       OP.mult, OP.add, reads=("pre2", CW.name, "sq"), writes=("sq",))
                em.stt('V', sq[:], pre[:, 4:4 + T], CW[:, n * 4 + 3:n * 4 + 4], sq[:],
                       OP.mult, OP.add, reads=("pre", CW.name, "sq"), writes=("sq",))
                dcol = vcol(n, 0) if isv else n * T
                em.act(dst[:, dcol:dcol + T], sq[:], AF.Silu, reads=("sq",), writes=(dkeys[n],))
        tap(f"q_t{tile}", q_s[:], QS)
        tap(f"k_t{tile}", k_s[:], KS)
        tap(f"v_t{tile}", v_s[:], VS)

        if stage < 3:
            break

        for (ti, src_s, dkeys) in [(0, q_s, QS), (1, k_s, KS)]:
            for h in range(H):
                c0, c1 = 2 * h, 2 * h + 1
                em.act(sq[:], src_s[:, c0 * T:(c0 + 1) * T], AF.Square,
                       reads=(dkeys[c0],), writes=("sq",))
                em.act(sq2[:], src_s[:, c1 * T:(c1 + 1) * T], AF.Square,
                       reads=(dkeys[c1],), writes=("sq2",))
                for b in range(NBLK):
                    col = 2 + ti * 4 + h
                    em.mm(Pps[b][:, col:col + 1], sq[:, b * 128:(b + 1) * 128], onesb[:, 0:1],
                          reads=("sq", onesb.name), writes=(PB[b],), start=True, stop=False)
                    em.mm(Pps[b][:, col:col + 1], sq2[:, b * 128:(b + 1) * 128], onesb[:, 0:1],
                          reads=("sq2", onesb.name), writes=(PB[b],), start=False, stop=True)
        em.fence('T', PB)
        for b in range(NBLK):
            em.copy('V', tok[:, 16 + b * 8:16 + b * 8 + 8], Pps[b][:, 2:10],
                    reads=(PB[b],), writes=("tok",))
        em.ts('V', tok[:, 16:48], tok[:, 16:48], 1e-12, OP.add, reads=("tok",), writes=("tok",))
        em.act(tok[:, 16:48], tok[:, 16:48], AF.Sqrt, reads=("tok",), writes=("tok",))
        em.op('V', 'reciprocal', tok[:, 16:48], tok[:, 16:48], reads=("tok",), writes=("tok",))
        for (ti, src_s, dkeys) in [(0, q_s, QS), (1, k_s, KS)]:
            for h in range(H):
                for b in range(NBLK):
                    col = 16 + b * 8 + ti * 4 + h
                    em.ts('V', diag[:], idb[:], tok[:, col:col + 1], OP.mult,
                          reads=(idb.name, "tok"), writes=("diag",))
                    em.mm(Pps[1][:, b * 128:(b + 1) * 128], onesb[:], diag[:],
                          reads=("diag", onesb.name), writes=(PB[1],), start=True, stop=True)
                em.copy('A', fbc[:], Pps[1][:, 0:T], reads=(PB[1],), writes=("fbc",))
                for cc in (2 * h, 2 * h + 1):
                    em.tt('V', src_s[:, cc * T:(cc + 1) * T], src_s[:, cc * T:(cc + 1) * T],
                          fbc[:], OP.mult, reads=(dkeys[cc], "fbc"), writes=(dkeys[cc],))
        tap(f"qn_t{tile}", q_s[:], QS)
        tap(f"kn_t{tile}", k_s[:], KS)

        if stage < 4:
            break

        for n in range(NCH):
            base = vcol(n, 0)
            for (KK, CW, dst, dkeys) in [(7, cl, lo_s, LOS), (25, cm, mi_s, MIS)]:
                for j in range(KK):
                    off = base - (KK - 1) + j
                    if j == 0:
                        em.ts('V', sq[:], v_s[:, off:off + T], CW[:, n * KK:n * KK + 1],
                              OP.mult, reads=(VS[n], CW.name), writes=("sq",))
                    else:
                        eng = 'V'
                        em.stt(eng, sq[:], v_s[:, off:off + T], CW[:, n * KK + j:n * KK + j + 1],
                               sq[:], OP.mult, OP.add, reads=(VS[n], CW.name, "sq"),
                               writes=("sq",))
                em.copy('V', dst[:, n * T:(n + 1) * T], sq[:], reads=("sq",), writes=(dkeys[n],))
            em.copy('V', v_s[:, base - 24:base], v_s[:, base + T - 24:base + T],
                    reads=(VS[n],), writes=(VS[n],))
        tap(f"lo_t{tile}", lo_s[:], LOS)
        tap(f"mi_t{tile}", mi_s[:], MIS)

        if stage < 5:
            break

        for n in range(NBLK):
            gn = tile * NBLK + n
            for h in range(H):
                c0, c1 = 2 * h, 2 * h + 1
                pb, pk, pbf = Pps[h], PB[h], Pbf[h]
                sl0 = slice(c0 * T + n * 128, c0 * T + (n + 1) * 128)
                sl1 = slice(c1 * T + n * 128, c1 * T + (n + 1) * 128)
                em.mm(pb[:, 0:128], k_s[:, sl0], k_s[:, sl0],
                      reads=(KS[c0],), writes=(pk,), start=True, stop=False)
                em.mm(pb[:, 0:128], k_s[:, sl1], k_s[:, sl1],
                      reads=(KS[c1],), writes=(pk,), start=False, stop=True)
                em.mm(pb[:, 128:256], k_s[:, sl0], q_s[:, sl0],
                      reads=(KS[c0], QS[c0]), writes=(pk,), start=True, stop=False)
                em.mm(pb[:, 128:256], k_s[:, sl1], q_s[:, sl1],
                      reads=(KS[c1], QS[c1]), writes=(pk,), start=False, stop=True)
                em.stt('V', dA[h][:], pb[:, 0:128], btok[:, n * 4 + h:n * 4 + h + 1], mlow[:],
                       OP.mult, OP.mult, reads=(pk, "btok", mlow.name), writes=(dA[h].name,))
                em.tt('V', dAttnT[h][:], pb[:, 128:256], mupi[:], OP.mult,
                      reads=(pk, mupi.name), writes=(dAttnT[h].name,))
                em.trans(pbf[:, 512:640], dA[h][:], idb[:], reads=(dA[h].name, idb.name),
                         writes=(pk,))
                em.copy('A', dAT[h][:], pbf[:, 512:640], reads=(pk,), writes=(dAT[h].name,))
                em.trans(pbf[:, 640:768], k_s[:, sl0], idb[:], reads=(KS[c0], idb.name),
                         writes=(pk,))
                em.trans(pbf[:, 768:896], k_s[:, sl1], idb[:], reads=(KS[c1], idb.name),
                         writes=(pk,))
                em.copy('A', dKtm[h][:], pbf[:, 640:896], reads=(pk,), writes=(dKtm[h].name,))
                em.trans(pbf[:, 640:768], v_s[:, vcol(c0, n * 128):vcol(c0, (n + 1) * 128)],
                         idb[:], reads=(VS[c0], idb.name), writes=(pk,))
                em.trans(pbf[:, 768:896], v_s[:, vcol(c1, n * 128):vcol(c1, (n + 1) * 128)],
                         idb[:], reads=(VS[c1], idb.name), writes=(pk,))
                em.ts('V', dVbtm[h][:], pbf[:, 640:896], btok[:, n * 4 + h:n * 4 + h + 1],
                      OP.mult, reads=(pk, "btok"), writes=(dVbtm[h].name,))
                em.ts('V', dKbtm[h][:], dKtm[h][:], btok[:, n * 4 + h:n * 4 + h + 1],
                      OP.mult, reads=(dKtm[h].name, "btok"), writes=(dKbtm[h].name,))
                em.tt('V', dRa[h][:], idb[:], dAT[h][:], OP.subtract,
                      reads=(idb.name, dAT[h].name), writes=(dRa[h].name,))
            Mc, MTc = dA, dAT
            Mn, MTn = dM, dMT
            rc, rn = dRa, dRb
            for lev in range(1, 7):
                for h in range(H):
                    pb, pk = Pps[h], PB[h]
                    em.mm(pb[:, 0:128], MTc[h][:], Mc[h][:],
                          reads=(MTc[h].name, Mc[h].name), writes=(pk,))
                    em.copy('A', Mn[h][:], pb[:, 0:128], reads=(pk,), writes=(Mn[h].name,))
                    if lev < 6:
                        em.mm(pb[:, 128:256], Mc[h][:], MTc[h][:],
                              reads=(MTc[h].name, Mc[h].name), writes=(pk,))
                        em.copy('A', MTn[h][:], pb[:, 128:256], reads=(pk,), writes=(MTn[h].name,))
                for h in range(H):
                    pb, pk = Pps[h], PB[h]
                    em.mm(pb[:, 256:384], Mn[h][:], rc[h][:],
                          reads=(Mn[h].name, rc[h].name), writes=(pk,))
                    em.tt('V', rn[h][:], rc[h][:], pb[:, 256:384], OP.add,
                          reads=(rc[h].name, pk), writes=(rn[h].name,))
                Mc, MTc = Mn, MTn
                Mn, MTn = (dM2, dMT2) if lev == 1 else ((dM, dMT) if lev % 2 == 0 else (dM2, dMT2))
                rc, rn = rn, rc
            TfT = rc
            for h in range(H):
                pb, pk = Pps[h], PB[h]
                em.mm(pb[:, 0:128], dKbtm[h][:, 0:128], TfT[h][:],
                      reads=(dKbtm[h].name, TfT[h].name), writes=(pk,), start=True, stop=True)
                em.mm(pb[:, 128:256], dKbtm[h][:, 128:256], TfT[h][:],
                      reads=(dKbtm[h].name, TfT[h].name), writes=(pk,), start=True, stop=True)
                em.ts('V', dW[h][:], pb[:, 0:256], -1.0, OP.mult, reads=(pk,),
                      writes=(dW[h].name,))
            for h in range(H):
                if gn > 0:
                    em.copy('A', dSbf[h][:], Sps[h][:, 0:512], reads=(SB[h],),
                            writes=(dSbf[h].name,))
            for h in range(H):
                pb, pk = Pps[h], PB[h]
                em.mm(pb[:, 0:256], TfT[h][:], dVbtm[h][:],
                      reads=(TfT[h].name, dVbtm[h].name), writes=(pk,),
                      start=True, stop=(gn == 0))
                if gn > 0:
                    for dkc in range(2):
                        em.mm(pb[:, 0:256], dW[h][:, dkc * 128:(dkc + 1) * 128],
                              dSbf[h][:, dkc * 256:(dkc + 1) * 256],
                              reads=(dW[h].name, dSbf[h].name), writes=(pk,),
                              start=False, stop=(dkc == 1))
            for h in range(H):
                em.copy('V', dU[h][:], Pps[h][:, 0:256], reads=(PB[h],), writes=(dU[h].name,))
            for h in range(H):
                pb, pk = Pps[h], PB[h]
                c0 = 2 * h
                for dvc in range(2):
                    po = pb[:, 256 + dvc * 128:384 + dvc * 128]
                    first = True
                    if gn > 0:
                        for dkc in range(2):
                            em.mm(po, dSbf[h][:, dkc * 256 + dvc * 128:dkc * 256 + (dvc + 1) * 128],
                                  q_s[:, (c0 + dkc) * T + n * 128:(c0 + dkc) * T + (n + 1) * 128],
                                  reads=(dSbf[h].name, QS[c0 + dkc]), writes=(pk,),
                                  start=first, stop=False)
                            first = False
                    em.mm(po, dU[h][:, dvc * 128:(dvc + 1) * 128], dAttnT[h][:],
                          reads=(dU[h].name, dAttnT[h].name), writes=(pk,),
                          start=first, stop=True)
            for h in range(H):
                c0 = 2 * h
                dst = do_s[:].rearrange("p (c t) -> p c t", c=NCH)[:, c0:c0 + 2,
                                                                  n * 128:(n + 1) * 128]
                em.copy('V', dst, Pps[h][:, 256:512].rearrange("p (c t) -> p c t", c=2),
                        reads=(PB[h],), writes=(DOS[c0], DOS[c0 + 1]))
            for h in range(H):
                for dkc in range(2):
                    em.mm(Sps[h][:, dkc * 256:(dkc + 1) * 256],
                          dKtm[h][:, dkc * 128:(dkc + 1) * 128], dU[h][:],
                          reads=(dKtm[h].name, dU[h].name), writes=(SB[h],),
                          start=(gn == 0 and dkc == 0), stop=(gn == L // C - 1 and dkc == 1))
        tap(f"do_t{tile}", do_s[:], DOS)

        if stage < 6:
            break

        for j in range(NJG):
            bi = j % 2
            pg, pgk = Pps[2], PB[2]
            pu, puk = Pps[3], PB[3]
            em.dma('S', wgs[bi][:], wg_b[:, j * 1024:(j + 1) * 1024],
                   reads=(f"wgs{bi}", "wg_b"), writes=(f"wgs{bi}",))
            em.dma('S', wus[bi][:], wu_b[:, j * 1024:(j + 1) * 1024],
                   reads=(f"wus{bi}", "wu_b"), writes=(f"wus{bi}",))
            em.dma('S', wds[bi][:], wd_b[:, j * 1024:(j + 1) * 1024],
                   reads=(f"wds{bi}", "wd_b"), writes=(f"wds{bi}",))
            for c in range(NCH):
                em.mm(pg[:, 0:T], wgs[bi][:, c * 128:(c + 1) * 128], xn[:, c * T:(c + 1) * T],
                      reads=(f"wgs{bi}", XN[c]), writes=(pgk,),
                      start=(c == 0), stop=(c == NCH - 1))
            for c in range(NCH):
                em.mm(pu[:, 0:T], wus[bi][:, c * 128:(c + 1) * 128], xn[:, c * T:(c + 1) * T],
                      reads=(f"wus{bi}", XN[c]), writes=(puk,),
                      start=(c == 0), stop=(c == NCH - 1))
            em.act(sg[:], pg[:, 0:T], AF.Silu, scale=1.0 / 127.0, reads=(pgk,), writes=("sg",))
            em.tt('V', hf[:], sg[:], pu[:, 0:T], OP.mult, reads=("sg", puk), writes=("hf",))
            for c in range(NCH):
                pd, pdk = (pg, pgk) if c % 2 == 0 else (pu, puk)
                em.mm(pd[:, 0:T], wds[bi][:, c * 128:(c + 1) * 128], hf[:],
                      reads=(f"wds{bi}", "hf"), writes=(pdk,), start=True, stop=True)
                em.stt('V', x_s[:, c * T:(c + 1) * T], pd[:, 0:T], FFN_SC,
                       x_s[:, c * T:(c + 1) * T], OP.mult, OP.add,
                       reads=(pdk, XS[c]), writes=(XS[c],))
        tap(f"xffn_t{tile}", x_s[:], XS)

        if stage < 7:
            break

        outsl = [(lo_s, LOS, False), (mi_s, MIS, False), (do_s, DOS, False), (v_s, VS, True)]
        for i_, (ss_, kk_, isv) in enumerate(outsl):
            for h in range(H):
                c0, c1 = 2 * h, 2 * h + 1
                a0 = vcol(c0, 0) if isv else c0 * T
                a1 = vcol(c1, 0) if isv else c1 * T
                for b in range(NBLK):
                    col = 10 + i_ * 4 + h
                    em.mm(Pps[b][:, col:col + 1], ss_[:, a0 + b * 128:a0 + (b + 1) * 128],
                          onesb[:, 0:1], reads=(kk_[c0], onesb.name), writes=(PB[b],),
                          start=True, stop=False)
                    em.mm(Pps[b][:, col:col + 1], ss_[:, a1 + b * 128:a1 + (b + 1) * 128],
                          onesb[:, 0:1], reads=(kk_[c1], onesb.name), writes=(PB[b],),
                          start=False, stop=True)
        pairs = [(0, 0), (1, 1), (2, 2), (3, 3), (0, 1), (0, 2), (0, 3), (1, 2), (1, 3), (2, 3)]
        for pi, (i_, j_) in enumerate(pairs):
            si, ki_, vi_ = outsl[i_]
            sjt, kj_, vj_ = outsl[j_]
            for h in range(H):
                for cc01 in range(2):
                    cc = 2 * h + cc01
                    ai = vcol(cc, 0) if vi_ else cc * T
                    aj = vcol(cc, 0) if vj_ else cc * T
                    dstsq = sq if cc01 == 0 else sq2
                    dk = "sq" if cc01 == 0 else "sq2"
                    if i_ == j_:
                        em.act(dstsq[:], si[:, ai:ai + T], AF.Square,
                               reads=(ki_[cc],), writes=(dk,))
                    else:
                        em.tt('V', dstsq[:], si[:, ai:ai + T], sjt[:, aj:aj + T], OP.mult,
                              reads=(ki_[cc], kj_[cc]), writes=(dk,))
                for b in range(NBLK):
                    col = 26 + pi * 4 + h
                    em.mm(Pps[b][:, col:col + 1], sq[:, b * 128:(b + 1) * 128], onesb[:, 0:1],
                          reads=("sq", onesb.name), writes=(PB[b],), start=True, stop=False)
                    em.mm(Pps[b][:, col:col + 1], sq2[:, b * 128:(b + 1) * 128], onesb[:, 0:1],
                          reads=("sq2", onesb.name), writes=(PB[b],), start=False, stop=True)
        em.fence('T', PB)
        for b in range(NBLK):
            mean_dst = ftmp[:, 0:32].rearrange("p (i g h) -> p i g h", g=2, h=4)[:, :, 0:1, :]
            var_dst = ftmp[:, 0:32].rearrange("p (i g h) -> p i g h", g=2, h=4)[:, :, 1:2, :]
            mean_src = Pps[b][:, 10:26].rearrange("p (i h) -> p i h", h=4)
            ss_src = Pps[b][:, 26:42].rearrange("p (i h) -> p i h", h=4)
            m3 = lambda t: t.rearrange("p (i h) -> p i h", h=4)
            em.ts('V', mean_dst, mean_src, 1.0 / DH, OP.mult, reads=(PB[b],), writes=("ftmp",))
            em.tt('V', m3(tokb[:, 0:16]), mean_dst, mean_dst, OP.mult,
                  reads=("ftmp",), writes=("tokb",))
            em.stt('V', m3(toke[:, 0:16]), ss_src, 1.0 / (DH - 1), m3(tokb[:, 0:16]),
                   OP.mult, OP.subtract, reads=(PB[b], "tokb"), writes=("toke",))
            em.stt('V', var_dst, m3(tokb[:, 0:16]), -1.0 / (DH - 1), m3(toke[:, 0:16]),
                   OP.mult, OP.add, reads=("tokb", "toke"), writes=("ftmp",))
            em.ts('V', ftmp[:, 32:56], Pps[b][:, 42:66], 1.0 / DH, OP.mult,
                  reads=(PB[b],), writes=("ftmp",))
            em.trans(Pps[b][0:56, 128:256], ftmp[:, 0:56], idf[:],
                     reads=("ftmp", idf.name), writes=(PB[b],))
            em.fence('T', (PB[b],))
            em.copy('V', feats[0:56, b * 128:(b + 1) * 128], Pps[b][0:56, 128:256],
                    reads=(PB[b],), writes=("feats",))
        tap(f"feats_t{tile}", feats[:], ("feats",))

        if stage < 8:
            break

        for nsl in range(17):
            bi = nsl % 2
            ph, phk = (Pps[0], PB[0]) if bi == 0 else (Pps[2], PB[2])
            em.dma('S', rw1s[bi][:], rw1_b[:, nsl * RK * 128:(nsl + 1) * RK * 128],
                   reads=(f"rw1s{bi}", "rw1_b"), writes=(f"rw1s{bi}",))
            for kc in range(RK):
                rhs = xn[:, kc * T:(kc + 1) * T] if kc < 8 else feats[0:128, 0:T]
                rk = XN[kc] if kc < 8 else "feats"
                em.mm(ph[:, 0:T], rw1s[bi][:, kc * 128:(kc + 1) * 128], rhs,
                      reads=(f"rw1s{bi}", rk), writes=(phk,),
                      start=(kc == 0), stop=(kc == RK - 1))
            h1buf = h1a if bi == 0 else h1b
            em.act(h1buf[:], ph[:, 0:T], AF.Silu, bias=rb1[:, nsl:nsl + 1],
                   reads=(phk, rb1.name), writes=(h1buf.name,))
            em.mm(Pps[1][0:16, 0:T], rw2[:, nsl * 16:(nsl + 1) * 16], h1buf[:],
                  reads=(rw2.name, h1buf.name), writes=(PB[1],),
                  start=(nsl == 0), stop=(nsl == 16))
        em.ts('V', lgt[0:16, :], Pps[1][0:16, 0:T], rb2[0:16, 0:1], OP.add,
              reads=(PB[1], rb2.name), writes=("lgt",))
        for b in range(NBLK):
            em.trans(Pps[3][0:128, b * 16:(b + 1) * 16], lgt[0:16, b * 128:(b + 1) * 128],
                     idf[0:16, 0:16], reads=("lgt", idf.name), writes=(PB[3],))
        em.fence('T', (PB[3],))
        em.copy('V', tokb[:, 0:64], Pps[3][:, 0:64], reads=(PB[3],), writes=("tokb",))
        em.op('V', 'tensor_reduce', out=tok[:, 0:16],
              in_=tokb[:, 0:64].rearrange("p (g i) -> p g i", i=4),
              axis=AX.X, op=OP.max, reads=("tokb",), writes=("tok",))
        for i_ in range(4):
            em.tt('V', toke[:, i_:64:4], tokb[:, i_:64:4], tok[:, 0:16], OP.subtract,
                  reads=("tokb", "tok"), writes=("toke",))
        em.act(toke[:, 0:64], toke[:, 0:64], AF.Tanh, scale=0.5, reads=("toke",),
               writes=("toke",))
        em.ts('V', tokb[:, 0:64], toke[:, 0:64], 1.0, OP.add, reads=("toke",), writes=("tokb",))
        em.ts('V', toke[:, 0:64], toke[:, 0:64], -1.0, OP.mult, 1.0, OP.add,
              reads=("toke",), writes=("toke",))
        em.op('V', 'reciprocal', toke[:, 0:64], toke[:, 0:64], reads=("toke",), writes=("toke",))
        em.tt('V', toke[:, 0:64], tokb[:, 0:64], toke[:, 0:64], OP.mult,
              reads=("tokb", "toke"), writes=("toke",))
        em.op('V', 'tensor_reduce', out=tok[:, 0:16],
              in_=toke[:, 0:64].rearrange("p (g i) -> p g i", i=4),
              axis=AX.X, op=OP.add, reads=("toke",), writes=("tok",))
        em.op('V', 'reciprocal', tok[:, 0:16], tok[:, 0:16], reads=("tok",), writes=("tok",))
        for i_ in range(4):
            em.tt('V', toke[:, i_:64:4], toke[:, i_:64:4], tok[:, 0:16], OP.mult,
                  reads=("toke", "tok"), writes=("toke",))
        em.ts('V', toke[:, 0:64], toke[:, 0:64], 0.96, OP.mult, 0.01, OP.add,
              reads=("toke",), writes=("toke",))
        tap(f"p_t{tile}", toke[:, 0:64], ("toke",))

        if stage < 9:
            break

        for h in range(H):
            cc0, cc1 = 2 * h, 2 * h + 1
            for i_, (ss_, kk_, isv) in enumerate(outsl):
                a0 = vcol(cc0, 0) if isv else cc0 * T
                a1 = vcol(cc1, 0) if isv else cc1 * T
                pbc, pbck = (Pps[2], PB[2]) if i_ % 2 == 0 else (Pps[3], PB[3])
                for b in range(NBLK):
                    col = b * 16 + h * 4 + i_
                    em.ts('V', diag[:], idb[:], toke[:, col:col + 1], OP.mult,
                          reads=(idb.name, "toke"), writes=("diag",))
                    em.mm(pbc[:, b * 128:(b + 1) * 128], onesb[:], diag[:],
                          reads=("diag", onesb.name), writes=(pbck,), start=True, stop=True)
                if i_ == 0:
                    em.tt('V', mxt[:], ss_[:, a0:a0 + T], pbc[:, 0:T], OP.mult,
                          reads=(kk_[cc0], pbck), writes=("mxt",))
                    em.tt('V', sg[:], ss_[:, a1:a1 + T], pbc[:, 0:T], OP.mult,
                          reads=(kk_[cc1], pbck), writes=("sg",))
                else:
                    em.tt('V', hf[:], ss_[:, a0:a0 + T], pbc[:, 0:T], OP.mult,
                          reads=(kk_[cc0], pbck), writes=("hf",))
                    em.tt('V', mxt[:], mxt[:], hf[:], OP.add,
                          reads=("mxt", "hf"), writes=("mxt",))
                    em.tt('V', hf[:], ss_[:, a1:a1 + T], pbc[:, 0:T], OP.mult,
                          reads=(kk_[cc1], pbck), writes=("hf",))
                    em.tt('V', sg[:], sg[:], hf[:], OP.add,
                          reads=("sg", "hf"), writes=("sg",))
            em.copy('V', lo_s[:, cc0 * T:(cc0 + 1) * T], mxt[:], reads=("mxt",),
                    writes=(LOS[cc0],))
            em.copy('V', lo_s[:, cc1 * T:(cc1 + 1) * T], sg[:], reads=("sg",),
                    writes=(LOS[cc1],))
            em.act(sq[:], mxt[:], AF.Square, reads=("mxt",), writes=("sq",))
            em.act(sq2[:], sg[:], AF.Square, reads=("sg",), writes=("sq2",))
            for b in range(NBLK):
                em.mm(Pps[b][:, 0:1], sq[:, b * 128:(b + 1) * 128], onesb[:, 0:1],
                      reads=("sq", onesb.name), writes=(PB[b],), start=True, stop=False)
                em.mm(Pps[b][:, 0:1], sq2[:, b * 128:(b + 1) * 128], onesb[:, 0:1],
                      reads=("sq2", onesb.name), writes=(PB[b],), start=False, stop=True)
            em.ts('V', sq[:], sq[:], mw2[:, 2 * h:2 * h + 1], OP.mult,
                  reads=("sq", mw2.name), writes=("sq",))
            em.ts('V', sq2[:], sq2[:], mw2[:, 2 * h + 1:2 * h + 2], OP.mult,
                  reads=("sq2", mw2.name), writes=("sq2",))
            for b in range(NBLK):
                em.mm(Pps[b][:, 1:2], sq[:, b * 128:(b + 1) * 128], onesb[:, 0:1],
                      reads=("sq", onesb.name), writes=(PB[b],), start=True, stop=False)
                em.mm(Pps[b][:, 1:2], sq2[:, b * 128:(b + 1) * 128], onesb[:, 0:1],
                      reads=("sq2", onesb.name), writes=(PB[b],), start=False, stop=True)
            em.fence('T', PB)
            for b in range(NBLK):
                em.copy('V', tok[:, 16 + b * 2:16 + b * 2 + 2], Pps[b][:, 0:2],
                        reads=(PB[b],), writes=("tok",))
            em.stt('V', tok[:, 24:28], tok[:, 16:24:2], 1e-5, tok[:, 17:24:2],
                   OP.mult, OP.add, reads=("tok",), writes=("tok",))
            em.ts('V', tok[:, 24:28], tok[:, 24:28], 1.0 / DH, OP.mult, 1e-10, OP.add,
                  reads=("tok",), writes=("tok",))
            em.act(tok[:, 24:28], tok[:, 24:28], AF.Sqrt, reads=("tok",), writes=("tok",))
            em.op('V', 'reciprocal', tok[:, 24:28], tok[:, 24:28], reads=("tok",), writes=("tok",))
            for b in range(NBLK):
                em.ts('V', diag[:], idb[:], tok[:, 24 + b:25 + b], OP.mult,
                      reads=(idb.name, "tok"), writes=("diag",))
                em.mm(Pps[3][:, b * 128:(b + 1) * 128], onesb[:], diag[:],
                      reads=("diag", onesb.name), writes=(PB[3],), start=True, stop=True)
            for cc01 in range(2):
                cc = 2 * h + cc01
                em.stt('V', k_s[:, cc * T:(cc + 1) * T], lo_s[:, cc * T:(cc + 1) * T],
                       w2[:, cc:cc + 1], Pps[3][:, 0:T], OP.mult, OP.mult,
                       reads=(LOS[cc], w2.name, PB[3]), writes=(KS[cc],))
        tap(f"on_t{tile}", k_s[:], KS)

        if stage < 10:
            break

        for nn in range(NCH):
            pb, pk = (Pps[0], PB[0]) if nn % 2 == 0 else (Pps[1], PB[1])
            for c in range(NCH):
                em.mm(pb[:, 0:T], wo[:, c * D + nn * 128:c * D + (nn + 1) * 128],
                      k_s[:, c * T:(c + 1) * T], reads=(wo.name, KS[c]), writes=(pk,),
                      start=(c == 0), stop=(c == NCH - 1))
            em.tt('V', x_s[:, nn * T:(nn + 1) * T], x_s[:, nn * T:(nn + 1) * T], pb[:, 0:T],
                  OP.add, reads=(XS[nn], pk), writes=(XS[nn],))
        dsto = out_h.rearrange("(c p) L -> p c L", p=128)[:, :, t0:t0 + T]
        em.dma('S', dsto, x_s[:].rearrange("p (c t) -> p c t", c=NCH),
               reads=tuple(XS), writes=())

    for q in 'SP':
        for i in range(em.NDL):
            if em.cntdl[(q, i)]:
                em.e[q].wait_ge(em.semd[q][i], 16 * em.cntdl[(q, i)])
    return nc, dbg


_PROG_CACHE = {}


def _host_inputs(inputs):
    f32 = np.float32

    def chunkw(w, nk, m):
        return np.ascontiguousarray(w.reshape(nk, 128, m).transpose(1, 0, 2).reshape(128, nk * m))

    rw1 = np.asarray(inputs["L0_attn_r_w1"], f32)
    rb1 = np.asarray(inputs["L0_attn_r_b1"], f32)
    rw2 = np.asarray(inputs["L0_attn_r_w2"], f32)
    rb2 = np.asarray(inputs["L0_attn_r_b2"], f32)
    rw1p = np.zeros((RK * 128, RH), f32)
    rw1p[:1080, :2160] = rw1
    rb1p = np.zeros((RH,), f32)
    rb1p[:2160] = rb1
    rw2p = np.zeros((RH, 16), f32)
    rw2p[:2160] = rw2
    rb2p = np.zeros((128, 1), f32)
    rb2p[:16, 0] = rb2

    def conv(nm, K):
        return chunkw(np.asarray(inputs[nm], f32), 8, K)

    wgT = np.ascontiguousarray(np.asarray(inputs["L0_ffn_w_gate"], np.int32).T)
    wuT = np.ascontiguousarray(np.asarray(inputs["L0_ffn_w_up"], np.int32).T)
    wdT = np.ascontiguousarray(np.asarray(inputs["L0_ffn_w_down"], np.int32).T)
    mwr = np.asarray(inputs["L0_attn_mix_w"], f32).reshape(4, 2, 128)
    mw_sb = np.ascontiguousarray(mwr.transpose(2, 0, 1).reshape(128, 8))
    onr = np.asarray(inputs["L0_attn_onorm_w"], f32).reshape(2, 128).T
    onorm_sb = np.ascontiguousarray(np.tile(onr, (1, 4)))
    return {
        "wq": chunkw(np.asarray(inputs["L0_attn_Wq"], f32), 8, D),
        "wk": chunkw(np.asarray(inputs["L0_attn_Wk"], f32), 8, D),
        "wv": chunkw(np.asarray(inputs["L0_attn_Wv"], f32), 8, D),
        "wb": chunkw(np.asarray(inputs["L0_attn_Wb"], f32), 8, 4),
        "wo": chunkw(np.asarray(inputs["L0_attn_Wo"], f32), 8, D),
        "rw1": chunkw(rw1p, RK, RH),
        "rb1": np.ascontiguousarray(rb1p.reshape(17, 128).T),
        "rw2": chunkw(rw2p, 17, 16), "rb2": rb2p,
        "cq": conv("L0_attn_conv_q_w", 4), "ck": conv("L0_attn_conv_k_w", 4),
        "cv": conv("L0_attn_conv_v_w", 4),
        "cl": conv("L0_attn_local_w", 7), "cm": conv("L0_attn_mid_w", 25),
        "mw": mw_sb, "onorm": onorm_sb,
        "wg": chunkw(wgT, 8, HID), "wu": chunkw(wuT, 8, HID),
        "wd": chunkw(wdT, NJG, 1024),
        "identbf": np.eye(128, dtype=BF).view(np.uint16),
        "identf32": np.eye(128, dtype=f32),
        "onesbf": np.ones((128, 128), BF).view(np.uint16),
        "onesf32": np.ones((128, 1), f32),
        "masklow": np.tril(np.ones((128, 128), f32), -1).astype(BF).view(np.uint16),
        "maskupincl": np.triu(np.ones((128, 128), f32), 0).astype(BF).view(np.uint16),
    }


def kernel(**inputs):
    global DEBUG_RESULTS
    taps = tuple(DEBUG_TAPS)
    key = (taps, STAGE, QKV_TENSORS, QKV_NX, QKV_CONV)
    if key not in _PROG_CACHE:
        _PROG_CACHE[key] = build_program(taps, STAGE)
    nc, dbg = _PROG_CACHE[key]
    shared = _host_inputs(inputs)
    x = np.asarray(inputs["L0_attn_hidden_states"], np.float32)
    in_maps = []
    for core in range(8):
        m = dict(shared)
        m["x"] = np.ascontiguousarray(x[core].T)
        in_maps.append(m)
    res = run_bass_kernel_spmd(nc, in_maps, core_ids=list(range(8)))
    out = np.stack([np.ascontiguousarray(np.asarray(r["out"]).T) for r in res.results])
    if taps:
        def cv(a, isbf):
            a = np.asarray(a)
            if isbf and a.dtype == np.uint16:
                return a.view(BF).astype(np.float32)
            if a.dtype == BF:
                return a.astype(np.float32)
            return a
        DEBUG_RESULTS = [{k: cv(r["dbg_" + k], isbf) for k, isbf in dbg.items()}
                         for r in res.results]
    return out


# version 05
# speedup vs baseline: 1.1074x; 1.1074x over previous
import numpy as np
import ml_dtypes
import concourse.bass as bass
import concourse.mybir as mybir
from concourse.bass_utils import run_bass_kernel_spmd

F32 = mybir.dt.float32
BF16 = mybir.dt.bfloat16
I32 = mybir.dt.int32
BF = ml_dtypes.bfloat16
AF = mybir.ActivationFunctionType
OP = mybir.AluOpType
AX = mybir.AxisListType

L, D, H, DH, T, NT, C = 4096, 1024, 4, 256, 512, 8, 128
NCH = 8
NBLK = 4
HID = 3072
RH = 2176
RK = 9
NJG = HID // 128
FFN_SC = 1.0 / (127.0 * 127.0)

DEBUG_TAPS = []
STAGE = 99
DEBUG_RESULTS = None
RISKY_DELAY = 0
QKV_TENSORS = 3
QKV_NX = 8
QKV_CONV = True


class Em:
    def __init__(self, nc):
        self.nc = nc
        self.e = {'T': nc.tensor, 'V': nc.vector, 'A': nc.scalar, 'P': nc.gpsimd, 'S': nc.sync}
        self.sem = {x: nc.alloc_semaphore(f"sem{x}") for x in 'TVAPS'}
        self.NDL = 8
        self.semd = {x: [nc.alloc_semaphore(f"semd{x}{i}") for i in range(self.NDL)]
                     for x in 'SP'}
        self.cnt = {x: 0 for x in 'TVAPS'}
        self.cntd = {x: 0 for x in 'SP'}
        self.cntdl = {(x, i): 0 for x in 'SP' for i in range(self.NDL)}
        self.water = {}
        self.state = {}

    def _wait(self, eng, key, val, risky=False):
        w = self.water.get((eng, key), 0)
        if w >= val:
            return
        if key[0] == 'd':
            sem = self.semd[key[1]][int(key[2:])]
        else:
            sem = self.sem[key]
        self.e[eng].wait_ge(sem, val)
        self.water[(eng, key)] = val
        if risky and RISKY_DELAY:
            self.e[eng].nop(cycle_cnt=RISKY_DELAY)

    def op(self, eng, method, *args, reads=(), writes=(), dma=False, risky=False, **kw):
        raw = {}
        war = {}
        for b in reads:
            st = self.state.get(b)
            if st is not None and st[0] is not None:
                k, v, rk = st[0]
                if raw.get(k, (0, False))[0] < v:
                    raw[k] = (v, rk)
        for b in writes:
            st = self.state.get(b)
            if st is not None:
                if st[0] is not None:
                    k, v, rk = st[0]
                    if war.get(k, 0) < v:
                        war[k] = v
                for k, v in st[1].items():
                    if war.get(k, 0) < v:
                        war[k] = v
        if dma:
            lane = self.cntd[eng] % self.NDL
            mykey = f"d{eng}{lane}"
        else:
            mykey = eng
        for k, (v, rk) in raw.items():
            if dma and k.startswith('d' + eng):
                continue
            self._wait(eng, k, v, risky=rk and k != eng)
        for k, v in war.items():
            if (dma and k.startswith('d' + eng)) or k == mykey or k == eng:
                continue
            self._wait(eng, k, v)
        if dma and self.cntdl[(eng, lane)] >= 1:
            self._wait(eng, mykey, 16 * self.cntdl[(eng, lane)])
        inst = getattr(self.e[eng], method)(*args, **kw)
        if dma:
            self.cntd[eng] += 1
            self.cntdl[(eng, lane)] += 1
            inst.then_inc(self.semd[eng][lane], 16)
            stamp = (mykey, 16 * self.cntdl[(eng, lane)], False)
        else:
            self.cnt[eng] += 1
            inst.then_inc(self.sem[eng], 1)
            stamp = (eng, self.cnt[eng], risky)
        for b in reads:
            st = self.state.get(b)
            if st is None:
                st = (None, {})
                self.state[b] = st
            st[1][stamp[0]] = stamp[1]
        for b in writes:
            self.state[b] = (stamp, {})
        return inst

    def mm(self, out, lhsT, rhs, *, reads, writes, start=True, stop=True, tile_position=None):
        try:
            nfree = int(out.free_size() if callable(out.free_size) else out.free_size)
        except Exception:
            nfree = 512
        return self.op('T', 'matmul', out, lhsT, rhs, start=start, stop=stop,
                       tile_position=tile_position, reads=reads, writes=writes,
                       risky=(nfree < 256))

    def trans(self, out, in_, ident, *, reads, writes):
        return self.op('T', 'transpose', out, in_, ident, reads=reads, writes=writes,
                       risky=True)

    def tt(self, eng, out, a, b, opc, *, reads, writes):
        return self.op(eng, 'tensor_tensor', out=out, in0=a, in1=b, op=opc,
                       reads=reads, writes=writes)

    def ts(self, eng, out, a, s1, op0, s2=None, op1=None, *, reads, writes):
        kw = dict(out=out, in0=a, scalar1=s1, scalar2=s2, op0=op0)
        if op1 is not None:
            kw['op1'] = op1
        return self.op(eng, 'tensor_scalar', reads=reads, writes=writes, **kw)

    def stt(self, eng, out, a, s, b, op0, op1, *, reads, writes):
        return self.op(eng, 'scalar_tensor_tensor', out=out, in0=a, scalar=s, in1=b,
                       op0=op0, op1=op1, reads=reads, writes=writes)

    def act(self, out, in_, func, *, reads, writes, bias=None, scale=None):
        kw = {}
        if bias is not None:
            kw['bias'] = bias
        if scale is not None:
            kw['scale'] = scale
        return self.op('A', 'activation', out, in_, func, reads=reads, writes=writes, **kw)

    def copy(self, eng, out, in_, *, reads, writes):
        if eng == 'A':
            return self.op('A', 'copy', out, in_, reads=reads, writes=writes)
        return self.op(eng, 'tensor_copy', out, in_, reads=reads, writes=writes)

    def dma(self, q, out, in_, *, reads, writes):
        return self.op(q, 'dma_start', out=out, in_=in_, dma=True, reads=reads, writes=writes)

    def fence(self, eng, keys):
        inst = self.e[eng].drain(fusable=False)
        self.cnt[eng] += 1
        inst.then_inc(self.sem[eng], 1)
        stamp = (eng, self.cnt[eng], False)
        for b in keys:
            st = self.state.get(b)
            if st is not None:
                self.state[b] = (stamp, dict(st[1]))
            else:
                self.state[b] = (stamp, {})


def build_program(taps=(), stage=99):
    nc = bass.Bass("TRN2", target_bir_lowering=False)
    em = Em(nc)
    dbg = {}

    def din(name, shape, dt):
        return nc.dram_tensor(name, shape, dt, kind="ExternalInput").ap()

    x_h = din("x", [D, L], F32)
    wq_h = din("wq", [128, 8 * D], F32)
    wk_h = din("wk", [128, 8 * D], F32)
    wv_h = din("wv", [128, 8 * D], F32)
    wb_h = din("wb", [128, 32], F32)
    wo_h = din("wo", [128, 8 * D], F32)
    rw1_h = din("rw1", [128, RK * RH], F32)
    rb1_h = din("rb1", [128, 17], F32)
    rw2_h = din("rw2", [128, 17 * 16], F32)
    rb2_h = din("rb2", [128, 1], F32)
    cq_h = din("cq", [128, 32], F32)
    ck_h = din("ck", [128, 32], F32)
    cv_h = din("cv", [128, 32], F32)
    cl_h = din("cl", [128, 56], F32)
    cm_h = din("cm", [128, 200], F32)
    mw_h = din("mw", [128, 8], F32)
    onorm_h = din("onorm", [128, 8], F32)
    wg_h = din("wg", [128, 8 * HID], I32)
    wu_h = din("wu", [128, 8 * HID], I32)
    wd_h = din("wd", [128, NJG * D], I32)
    id_h = din("identbf", [128, 128], BF16)
    idf_h = din("identf32", [128, 128], F32)
    ones_h = din("onesbf", [128, 128], BF16)
    onesf_h = din("onesf32", [128, 1], F32)
    mlow_h = din("masklow", [128, 128], BF16)
    mupi_h = din("maskupincl", [128, 128], BF16)

    out_h = nc.dram_tensor("out", [D, L], F32, kind="ExternalOutput").ap()

    wg_b = nc.dram_tensor("wg_bf", [128, NJG * 1024], BF16, kind="Internal").ap()
    wu_b = nc.dram_tensor("wu_bf", [128, NJG * 1024], BF16, kind="Internal").ap()
    wd_b = nc.dram_tensor("wd_bf", [128, NJG * 1024], BF16, kind="Internal").ap()
    rw1_b = nc.dram_tensor("rw1_bf", [128, 17 * RK * 128], BF16, kind="Internal").ap()

    def sb(name, shape, dt):
        return nc.alloc_sbuf_tensor(name, shape, dt)

    wq = sb("wq_s", [128, 8 * D], BF16)
    wk = sb("wk_s", [128, 8 * D], BF16)
    wv = sb("wv_s", [128, 8 * D], BF16)
    wb = sb("wb_s", [128, 32], BF16)
    wo = sb("wo_s", [128, 8 * D], BF16)
    rw2 = sb("rw2_s", [128, 17 * 16], BF16)
    rb1 = sb("rb1_s", [128, 17], F32)
    rb2 = sb("rb2_s", [128, 1], F32)
    cq = sb("cq_s", [128, 32], F32)
    ck = sb("ck_s", [128, 32], F32)
    cv = sb("cv_s", [128, 32], F32)
    cl = sb("cl_s", [128, 56], F32)
    cm = sb("cm_s", [128, 200], F32)
    mw = sb("mw_s", [128, 8], F32)
    onorm = sb("onorm_s", [128, 8], F32)
    w2 = sb("w2_s", [128, 8], F32)
    mw2 = sb("mw2_s", [128, 8], F32)
    idb = sb("idb_s", [128, 128], BF16)
    idf = sb("idf_s", [128, 128], F32)
    onesb = sb("onesb_s", [128, 128], BF16)
    onesf = sb("onesf_s", [128, 1], F32)
    mlow = sb("mlow_s", [128, 128], BF16)
    mupi = sb("mupi_s", [128, 128], BF16)

    x_s = sb("x_s", [128, NCH * T], F32)
    xn = sb("xn_s", [128, NCH * T], BF16)
    q_s = sb("q_s", [128, NCH * T], BF16)
    k_s = sb("k_s", [128, NCH * T], BF16)
    v_s = sb("v_s", [128, NCH * (24 + T)], BF16)
    do_s = sb("do_s", [128, NCH * T], BF16)
    lo_s = sb("lo_s", [128, NCH * T], BF16)
    mi_s = sb("mi_s", [128, NCH * T], BF16)
    pre = sb("pre_s", [128, 520], BF16)
    pre2 = sb("pre2_s", [128, 520], BF16)
    carry = sb("carry_s", [128, 3 * NCH * 3], BF16)
    sq = sb("sq_s", [128, T], BF16)
    sq2 = sb("sq2_s", [128, T], BF16)
    mxt = sb("mxt_s", [128, T], BF16)
    tok = sb("tok_s", [128, 64], F32)
    tokb = sb("tokb_s", [128, 64], F32)
    toke = sb("toke_s", [128, 64], F32)
    ftmp = sb("ftmp_s", [128, 64], F32)
    mbc = sb("mbc_s", [128, T], BF16)
    rbc = sb("rbc_s", [128, T], BF16)
    fbc = sb("fbc_s", [128, T], BF16)
    diag = sb("diag_s", [128, 128], BF16)
    btok = sb("btok_s", [128, 16], F32)
    lgt = sb("lgt_s", [16, T], F32)
    feats = sb("feats_s", [128, T], BF16)
    h1a = sb("h1a_s", [128, T], BF16)
    h1b = sb("h1b_s", [128, T], BF16)
    rw1s = [sb(f"rw1s{i}", [128, RK * 128], BF16) for i in range(2)]
    wgs = [sb(f"wgs{i}", [128, 1024], BF16) for i in range(2)]
    wus = [sb(f"wus{i}", [128, 1024], BF16) for i in range(2)]
    wds = [sb(f"wds{i}", [128, 1024], BF16) for i in range(2)]
    sg = sb("sg_s", [128, T], BF16)
    hf = sb("hf_s", [128, T], BF16)
    bigp = sb("bigp_s", [128, 1024], BF16)

    dA = [sb(f"dA{h}", [128, 128], BF16) for h in range(H)]
    dAT = [sb(f"dAT{h}", [128, 128], BF16) for h in range(H)]
    dM = [sb(f"dM{h}", [128, 128], BF16) for h in range(H)]
    dMT = [sb(f"dMT{h}", [128, 128], BF16) for h in range(H)]
    dM2 = [sb(f"dM2{h}", [128, 128], BF16) for h in range(H)]
    dMT2 = [sb(f"dMT2{h}", [128, 128], BF16) for h in range(H)]
    dRa = [sb(f"dRa{h}", [128, 128], BF16) for h in range(H)]
    dRb = [sb(f"dRb{h}", [128, 128], BF16) for h in range(H)]
    dAttnT = [sb(f"dAttnT{h}", [128, 128], BF16) for h in range(H)]
    dKtm = [sb(f"dKtm{h}", [128, 256], BF16) for h in range(H)]
    dKbtm = [sb(f"dKbtm{h}", [128, 256], BF16) for h in range(H)]
    dVbtm = [sb(f"dVbtm{h}", [128, 256], BF16) for h in range(H)]
    dU = [sb(f"dU{h}", [128, 256], BF16) for h in range(H)]
    dW = [sb(f"dW{h}", [128, 256], BF16) for h in range(H)]
    dSbf = [sb(f"dSbf{h}", [128, 512], BF16) for h in range(H)]

    Sps = [nc.alloc_psum_tensor(f"Sps{h}", [128, 512], F32) for h in range(H)]
    Pps = [nc.alloc_psum_tensor(f"Pps{i}", [128, 512], F32) for i in range(4)]
    Pbf = [p.ap().bitcast(BF16) for p in Pps]

    SB = [f"SB{h}" for h in range(H)]
    PB = [f"PB{i}" for i in range(4)]

    XS = [f"x{c}" for c in range(NCH)]
    XN = [f"xn{c}" for c in range(NCH)]
    QS = [f"q{c}" for c in range(NCH)]
    KS = [f"k{c}" for c in range(NCH)]
    VS = [f"v{c}" for c in range(NCH)]
    DOS = [f"do{c}" for c in range(NCH)]
    LOS = [f"lo{c}" for c in range(NCH)]
    MIS = [f"mi{c}" for c in range(NCH)]

    def vcol(c, t):
        return c * (24 + T) + 24 + t

    def tap(name, region, reads):
        if name not in taps:
            return
        shp = list(region.shape)
        dt = region.dtype
        o = nc.dram_tensor("dbg_" + name, shp, dt, kind="ExternalOutput").ap()
        em.dma('S', o, region, reads=tuple(reads), writes=())
        dbg[name] = (dt == BF16)

    for dst, src in [(idb, id_h), (idf, idf_h), (onesb, ones_h), (onesf, onesf_h),
                     (mlow, mlow_h), (mupi, mupi_h), (rb1, rb1_h), (rb2, rb2_h),
                     (cq, cq_h), (ck, ck_h), (cv, cv_h), (cl, cl_h), (cm, cm_h),
                     (mw, mw_h), (onorm, onorm_h)]:
        em.dma('S', dst[:], src, reads=(), writes=(dst.name,))
    for dst, src in [(wq, wq_h), (wk, wk_h), (wv, wv_h), (wb, wb_h), (wo, wo_h), (rw2, rw2_h)]:
        em.dma('P', dst[:], src, reads=(), writes=(dst.name,))
    for (srch, dstd, key) in [(wg_h, wg_b, "wg_b"), (wu_h, wu_b, "wu_b")]:
        for kk in range(8):
            for pp in range(3):
                em.dma('P', bigp[:, :1024], srch[:, kk * HID + pp * 1024:kk * HID + (pp + 1) * 1024],
                       reads=("bigp",), writes=("bigp",))
                dst3 = dstd.rearrange("p (j x) -> p j x", x=1024)[:, pp * 8:(pp + 1) * 8,
                                                                 kk * 128:(kk + 1) * 128]
                src3 = bigp[:, :1024].rearrange("p (j c) -> p j c", c=128)
                em.dma('S', dst3, src3, reads=("bigp",), writes=(key,))
    for jj in range(NJG):
        em.dma('P', bigp[:, :1024], wd_h[:, jj * 1024:(jj + 1) * 1024],
               reads=("bigp",), writes=("bigp",))
        em.dma('S', wd_b[:, jj * 1024:(jj + 1) * 1024], bigp[:, :1024],
               reads=("bigp",), writes=("wd_b",))
    for kk in range(RK):
        for pp in range(4):
            w = 512 if pp < 3 else 640
            o = pp * 512
            nsl = w // 128
            em.dma('P', bigp[:, :w], rw1_h[:, kk * RH + o:kk * RH + o + w],
                   reads=("bigp",), writes=("bigp",))
            dst3 = rw1_b.rearrange("p (n x) -> p n x", x=RK * 128)[:, pp * 4:pp * 4 + nsl,
                                                                   kk * 128:(kk + 1) * 128]
            src3 = bigp[:, :w].rearrange("p (n j) -> p n j", j=128)
            em.dma('S', dst3, src3, reads=("bigp",), writes=("rw1_b",))

    em.tt('V', w2[:], mw[:], onorm[:], OP.mult, reads=(mw.name, onorm.name), writes=(w2.name,))
    em.tt('V', mw2[:], mw[:], mw[:], OP.mult, reads=(mw.name,), writes=(mw2.name,))
    em.op('P', 'memset', feats[:], 0.0, reads=(), writes=("feats",))
    em.op('P', 'memset', carry[:], 0.0, reads=(), writes=("carry",))
    em.op('P', 'memset', v_s[:], 0.0, reads=(), writes=tuple(VS))

    for tile in range(NT):
        t0 = tile * T

        src3 = x_h.rearrange("(c p) L -> p c L", p=128)[:, :, t0:t0 + T]
        dst3 = x_s[:].rearrange("p (c t) -> p c t", c=NCH)
        em.dma('S', dst3, src3, reads=tuple(XS), writes=tuple(XS))

        for c in range(NCH):
            for b in range(NBLK):
                em.mm(Pps[b][:, 0:1], x_s[:, c * T + b * 128:c * T + (b + 1) * 128],
                      onesf[:], reads=(XS[c], onesf.name), writes=(PB[b],),
                      start=(c == 0), stop=(c == NCH - 1))
        for c in range(NCH):
            em.act(sq[:], x_s[:, c * T:(c + 1) * T], AF.Square, reads=(XS[c],), writes=("sq",))
            for b in range(NBLK):
                em.mm(Pps[b][:, 1:2], sq[:, b * 128:(b + 1) * 128], onesb[:, 0:1],
                      reads=("sq", onesb.name), writes=(PB[b],),
                      start=(c == 0), stop=(c == NCH - 1))
        em.fence('T', PB)
        for b in range(NBLK):
            em.copy('V', tok[:, b:b + 1], Pps[b][:, 0:1], reads=(PB[b],), writes=("tok",))
            em.copy('V', tok[:, 4 + b:5 + b], Pps[b][:, 1:2], reads=(PB[b],), writes=("tok",))
        for b in range(NBLK):
            em.copy('V', tok[:, 32 + b:33 + b], Pps[b][:, 0:1], reads=(PB[b],), writes=("tok",))
            em.copy('V', tok[:, 36 + b:37 + b], Pps[b][:, 1:2], reads=(PB[b],), writes=("tok",))
        tap(f"tokraw_t{tile}", tok[:], ("tok",))
        em.ts('V', tok[:, 0:8], tok[:, 0:8], 1.0 / D, OP.mult, reads=("tok",), writes=("tok",))
        em.tt('V', tokb[:, 0:4], tok[:, 0:4], tok[:, 0:4], OP.mult,
              reads=("tok",), writes=("tokb",))
        em.stt('V', tok[:, 8:12], tok[:, 4:8], 1e-5, tokb[:, 0:4], OP.add, OP.subtract,
               reads=("tok", "tokb"), writes=("tok",))
        em.copy('V', tok[:, 40:44], tok[:, 8:12], reads=("tok",), writes=("tok",))
        em.act(tok[:, 8:12], tok[:, 8:12], AF.Sqrt, reads=("tok",), writes=("tok",))
        em.copy('V', tok[:, 44:48], tok[:, 8:12], reads=("tok",), writes=("tok",))
        em.op('V', 'reciprocal', tok[:, 12:16], tok[:, 8:12], reads=("tok",), writes=("tok",))
        for b in range(NBLK):
            em.ts('V', diag[:], idb[:], tok[:, 12 + b:13 + b], OP.mult,
                  reads=(idb.name, "tok"), writes=("diag",))
            em.mm(Pps[1][:, b * 128:(b + 1) * 128], onesb[:], diag[:],
                  reads=("diag", onesb.name), writes=(PB[1],), start=True, stop=True)
            em.ts('V', diag[:], idb[:], tok[:, b:b + 1], OP.mult,
                  reads=(idb.name, "tok"), writes=("diag",))
            em.mm(Pps[2][:, b * 128:(b + 1) * 128], onesb[:], diag[:],
                  reads=("diag", onesb.name), writes=(PB[2],), start=True, stop=True)
        em.copy('A', rbc[:], Pps[1][:, 0:T], reads=(PB[1],), writes=("rbc",))
        em.copy('A', mbc[:], Pps[2][:, 0:T], reads=(PB[2],), writes=("mbc",))
        tap(f"tok_t{tile}", tok[:], ("tok",))
        tap(f"mbc_t{tile}", mbc[:], ("mbc",))
        tap(f"rbc_t{tile}", rbc[:], ("rbc",))
        for c in range(NCH):
            em.tt('V', mxt[:], x_s[:, c * T:(c + 1) * T], mbc[:], OP.subtract,
                  reads=(XS[c], "mbc"), writes=("mxt",))
            em.tt('V', xn[:, c * T:(c + 1) * T], mxt[:], rbc[:], OP.mult,
                  reads=("mxt", "rbc"), writes=(XN[c],))
        tap(f"xn_t{tile}", xn[:], XN)

        if stage < 1:
            break

        for c in range(NCH):
            em.mm(Pps[3][0:4, 0:T], wb[:, c * 4:(c + 1) * 4], xn[:, c * T:(c + 1) * T],
                  reads=(wb.name, XN[c]), writes=(PB[3],), start=(c == 0), stop=(c == NCH - 1))
        em.copy('V', lgt[0:4, :], Pps[3][0:4, 0:T], reads=(PB[3],), writes=("lgt",))
        for b in range(NBLK):
            em.trans(Pps[0][0:128, b * 4:(b + 1) * 4], lgt[0:4, b * 128:(b + 1) * 128],
                     idf[0:4, 0:4], reads=("lgt", idf.name), writes=(PB[0],))
        em.fence('T', (PB[0],))
        em.act(btok[:], Pps[0][:, 0:16], AF.Tanh, scale=0.5, reads=(PB[0],), writes=("btok",))
        em.ts('V', btok[:], btok[:], 0.5, OP.mult, 0.5, OP.add, reads=("btok",), writes=("btok",))
        tap(f"btok_t{tile}", btok[:], ("btok",))

        if stage < 2:
            break

        for (X, W_, CW, dst, dkeys, isv) in [
                (0, wq, cq, q_s, QS, False), (1, wk, ck, k_s, KS, False),
                (2, wv, cv, v_s, VS, True)][:QKV_TENSORS]:
            for n in range(NCH)[:QKV_NX]:
                pb, pk = Pps[n % 2], PB[n % 2]
                for c in range(NCH):
                    em.mm(pb[:, 0:T], W_[:, c * D + n * 128:c * D + (n + 1) * 128],
                          xn[:, c * T:(c + 1) * T], reads=(W_.name, XN[c]), writes=(pk,),
                          start=(c == 0), stop=(c == NCH - 1))
                ci = X * NCH + n
                em.copy('V', pre[:, 1:4], carry[:, ci * 3:ci * 3 + 3],
                        reads=("carry",), writes=("pre",))
                em.copy('A', pre[:, 4:4 + T], pb[:, 0:T], reads=(pk,), writes=("pre",))
                em.copy('V', carry[:, ci * 3:ci * 3 + 3], pre[:, 1 + T:4 + T],
                        reads=("pre",), writes=("carry",))
                em.copy('V', pre2[:, 0:3 + T], pre[:, 1:4 + T], reads=("pre",), writes=("pre2",))
                if not QKV_CONV:
                    em.act(dst[:, (vcol(n, 0) if isv else n * T):(vcol(n, 0) if isv else n * T) + T],
                           pre[:, 4:4 + T], AF.Silu, reads=("pre",), writes=(dkeys[n],))
                    continue
                em.ts('V', sq[:], pre2[:, 0:T], CW[:, n * 4:n * 4 + 1], OP.mult,
                      reads=("pre2", CW.name), writes=("sq",))
                em.stt('V', sq[:], pre[:, 2:2 + T], CW[:, n * 4 + 1:n * 4 + 2], sq[:],
                       OP.mult, OP.add, reads=("pre", CW.name, "sq"), writes=("sq",))
                em.stt('V', sq[:], pre2[:, 2:2 + T], CW[:, n * 4 + 2:n * 4 + 3], sq[:],
                       OP.mult, OP.add, reads=("pre2", CW.name, "sq"), writes=("sq",))
                em.stt('V', sq[:], pre[:, 4:4 + T], CW[:, n * 4 + 3:n * 4 + 4], sq[:],
                       OP.mult, OP.add, reads=("pre", CW.name, "sq"), writes=("sq",))
                dcol = vcol(n, 0) if isv else n * T
                em.act(dst[:, dcol:dcol + T], sq[:], AF.Silu, reads=("sq",), writes=(dkeys[n],))
        tap(f"q_t{tile}", q_s[:], QS)
        tap(f"k_t{tile}", k_s[:], KS)
        tap(f"v_t{tile}", v_s[:], VS)

        if stage < 3:
            break

        for (ti, src_s, dkeys) in [(0, q_s, QS), (1, k_s, KS)]:
            for h in range(H):
                c0, c1 = 2 * h, 2 * h + 1
                em.act(sq[:], src_s[:, c0 * T:(c0 + 1) * T], AF.Square,
                       reads=(dkeys[c0],), writes=("sq",))
                em.act(sq2[:], src_s[:, c1 * T:(c1 + 1) * T], AF.Square,
                       reads=(dkeys[c1],), writes=("sq2",))
                for b in range(NBLK):
                    col = 2 + ti * 4 + h
                    em.mm(Pps[b][:, col:col + 1], sq[:, b * 128:(b + 1) * 128], onesb[:, 0:1],
                          reads=("sq", onesb.name), writes=(PB[b],), start=True, stop=False)
                    em.mm(Pps[b][:, col:col + 1], sq2[:, b * 128:(b + 1) * 128], onesb[:, 0:1],
                          reads=("sq2", onesb.name), writes=(PB[b],), start=False, stop=True)
        em.fence('T', PB)
        for b in range(NBLK):
            em.copy('V', tok[:, 16 + b * 8:16 + b * 8 + 8], Pps[b][:, 2:10],
                    reads=(PB[b],), writes=("tok",))
        em.ts('V', tok[:, 16:48], tok[:, 16:48], 1e-12, OP.add, reads=("tok",), writes=("tok",))
        em.act(tok[:, 16:48], tok[:, 16:48], AF.Sqrt, reads=("tok",), writes=("tok",))
        em.op('V', 'reciprocal', tok[:, 16:48], tok[:, 16:48], reads=("tok",), writes=("tok",))
        for (ti, src_s, dkeys) in [(0, q_s, QS), (1, k_s, KS)]:
            for h in range(H):
                for b in range(NBLK):
                    col = 16 + b * 8 + ti * 4 + h
                    em.ts('V', diag[:], idb[:], tok[:, col:col + 1], OP.mult,
                          reads=(idb.name, "tok"), writes=("diag",))
                    em.mm(Pps[1][:, b * 128:(b + 1) * 128], onesb[:], diag[:],
                          reads=("diag", onesb.name), writes=(PB[1],), start=True, stop=True)
                em.copy('A', fbc[:], Pps[1][:, 0:T], reads=(PB[1],), writes=("fbc",))
                for cc in (2 * h, 2 * h + 1):
                    em.tt('V', src_s[:, cc * T:(cc + 1) * T], src_s[:, cc * T:(cc + 1) * T],
                          fbc[:], OP.mult, reads=(dkeys[cc], "fbc"), writes=(dkeys[cc],))
        tap(f"qn_t{tile}", q_s[:], QS)
        tap(f"kn_t{tile}", k_s[:], KS)

        if stage < 4:
            break

        for n in range(NCH):
            base = vcol(n, 0)
            for (KK, CW, dst, dkeys) in [(7, cl, lo_s, LOS), (25, cm, mi_s, MIS)]:
                for j in range(KK):
                    off = base - (KK - 1) + j
                    if j == 0:
                        em.ts('V', sq[:], v_s[:, off:off + T], CW[:, n * KK:n * KK + 1],
                              OP.mult, reads=(VS[n], CW.name), writes=("sq",))
                    else:
                        eng = 'V'
                        em.stt(eng, sq[:], v_s[:, off:off + T], CW[:, n * KK + j:n * KK + j + 1],
                               sq[:], OP.mult, OP.add, reads=(VS[n], CW.name, "sq"),
                               writes=("sq",))
                em.copy('V', dst[:, n * T:(n + 1) * T], sq[:], reads=("sq",), writes=(dkeys[n],))
            em.copy('V', v_s[:, base - 24:base], v_s[:, base + T - 24:base + T],
                    reads=(VS[n],), writes=(VS[n],))
        tap(f"lo_t{tile}", lo_s[:], LOS)
        tap(f"mi_t{tile}", mi_s[:], MIS)

        if stage < 5:
            break

        for n in range(NBLK):
            gn = tile * NBLK + n
            for h in range(H):
                c0, c1 = 2 * h, 2 * h + 1
                pb, pk, pbf = Pps[h], PB[h], Pbf[h]
                sl0 = slice(c0 * T + n * 128, c0 * T + (n + 1) * 128)
                sl1 = slice(c1 * T + n * 128, c1 * T + (n + 1) * 128)
                em.mm(pb[:, 0:128], k_s[:, sl0], k_s[:, sl0],
                      reads=(KS[c0],), writes=(pk,), start=True, stop=False)
                em.mm(pb[:, 0:128], k_s[:, sl1], k_s[:, sl1],
                      reads=(KS[c1],), writes=(pk,), start=False, stop=True)
                em.mm(pb[:, 128:256], k_s[:, sl0], q_s[:, sl0],
                      reads=(KS[c0], QS[c0]), writes=(pk,), start=True, stop=False)
                em.mm(pb[:, 128:256], k_s[:, sl1], q_s[:, sl1],
                      reads=(KS[c1], QS[c1]), writes=(pk,), start=False, stop=True)
                em.stt('V', dA[h][:], pb[:, 0:128], btok[:, n * 4 + h:n * 4 + h + 1], mlow[:],
                       OP.mult, OP.mult, reads=(pk, "btok", mlow.name), writes=(dA[h].name,))
                em.tt('V', dAttnT[h][:], pb[:, 128:256], mupi[:], OP.mult,
                      reads=(pk, mupi.name), writes=(dAttnT[h].name,))
                em.trans(pbf[:, 512:640], dA[h][:], idb[:], reads=(dA[h].name, idb.name),
                         writes=(pk,))
                em.copy('A', dAT[h][:], pbf[:, 512:640], reads=(pk,), writes=(dAT[h].name,))
                em.trans(pbf[:, 640:768], k_s[:, sl0], idb[:], reads=(KS[c0], idb.name),
                         writes=(pk,))
                em.trans(pbf[:, 768:896], k_s[:, sl1], idb[:], reads=(KS[c1], idb.name),
                         writes=(pk,))
                em.copy('A', dKtm[h][:], pbf[:, 640:896], reads=(pk,), writes=(dKtm[h].name,))
                em.trans(pbf[:, 640:768], v_s[:, vcol(c0, n * 128):vcol(c0, (n + 1) * 128)],
                         idb[:], reads=(VS[c0], idb.name), writes=(pk,))
                em.trans(pbf[:, 768:896], v_s[:, vcol(c1, n * 128):vcol(c1, (n + 1) * 128)],
                         idb[:], reads=(VS[c1], idb.name), writes=(pk,))
                em.ts('V', dVbtm[h][:], pbf[:, 640:896], btok[:, n * 4 + h:n * 4 + h + 1],
                      OP.mult, reads=(pk, "btok"), writes=(dVbtm[h].name,))
                em.ts('V', dKbtm[h][:], dKtm[h][:], btok[:, n * 4 + h:n * 4 + h + 1],
                      OP.mult, reads=(dKtm[h].name, "btok"), writes=(dKbtm[h].name,))
                em.tt('V', dRa[h][:], idb[:], dAT[h][:], OP.subtract,
                      reads=(idb.name, dAT[h].name), writes=(dRa[h].name,))
            Mc, MTc = dA, dAT
            Mn, MTn = dM, dMT
            rc, rn = dRa, dRb
            for lev in range(1, 7):
                for h in range(H):
                    pb, pk = Pps[h], PB[h]
                    em.mm(pb[:, 0:128], MTc[h][:], Mc[h][:],
                          reads=(MTc[h].name, Mc[h].name), writes=(pk,))
                    em.copy('A', Mn[h][:], pb[:, 0:128], reads=(pk,), writes=(Mn[h].name,))
                    if lev < 6:
                        em.mm(pb[:, 128:256], Mc[h][:], MTc[h][:],
                              reads=(MTc[h].name, Mc[h].name), writes=(pk,))
                        em.copy('A', MTn[h][:], pb[:, 128:256], reads=(pk,), writes=(MTn[h].name,))
                for h in range(H):
                    pb, pk = Pps[h], PB[h]
                    em.mm(pb[:, 256:384], Mn[h][:], rc[h][:],
                          reads=(Mn[h].name, rc[h].name), writes=(pk,))
                    em.tt('V', rn[h][:], rc[h][:], pb[:, 256:384], OP.add,
                          reads=(rc[h].name, pk), writes=(rn[h].name,))
                Mc, MTc = Mn, MTn
                Mn, MTn = (dM2, dMT2) if lev == 1 else ((dM, dMT) if lev % 2 == 0 else (dM2, dMT2))
                rc, rn = rn, rc
            TfT = rc
            for h in range(H):
                pb, pk = Pps[h], PB[h]
                em.mm(pb[:, 0:128], dKbtm[h][:, 0:128], TfT[h][:],
                      reads=(dKbtm[h].name, TfT[h].name), writes=(pk,), start=True, stop=True)
                em.mm(pb[:, 128:256], dKbtm[h][:, 128:256], TfT[h][:],
                      reads=(dKbtm[h].name, TfT[h].name), writes=(pk,), start=True, stop=True)
                em.ts('V', dW[h][:], pb[:, 0:256], -1.0, OP.mult, reads=(pk,),
                      writes=(dW[h].name,))
            for h in range(H):
                if gn > 0:
                    em.copy('A', dSbf[h][:], Sps[h][:, 0:512], reads=(SB[h],),
                            writes=(dSbf[h].name,))
            for h in range(H):
                pb, pk = Pps[h], PB[h]
                em.mm(pb[:, 0:256], TfT[h][:], dVbtm[h][:],
                      reads=(TfT[h].name, dVbtm[h].name), writes=(pk,),
                      start=True, stop=(gn == 0))
                if gn > 0:
                    for dkc in range(2):
                        em.mm(pb[:, 0:256], dW[h][:, dkc * 128:(dkc + 1) * 128],
                              dSbf[h][:, dkc * 256:(dkc + 1) * 256],
                              reads=(dW[h].name, dSbf[h].name), writes=(pk,),
                              start=False, stop=(dkc == 1))
            for h in range(H):
                em.copy('V', dU[h][:], Pps[h][:, 0:256], reads=(PB[h],), writes=(dU[h].name,))
            for h in range(H):
                pb, pk = Pps[h], PB[h]
                c0 = 2 * h
                for dvc in range(2):
                    po = pb[:, 256 + dvc * 128:384 + dvc * 128]
                    first = True
                    if gn > 0:
                        for dkc in range(2):
                            em.mm(po, dSbf[h][:, dkc * 256 + dvc * 128:dkc * 256 + (dvc + 1) * 128],
                                  q_s[:, (c0 + dkc) * T + n * 128:(c0 + dkc) * T + (n + 1) * 128],
                                  reads=(dSbf[h].name, QS[c0 + dkc]), writes=(pk,),
                                  start=first, stop=False)
                            first = False
                    em.mm(po, dU[h][:, dvc * 128:(dvc + 1) * 128], dAttnT[h][:],
                          reads=(dU[h].name, dAttnT[h].name), writes=(pk,),
                          start=first, stop=True)
            for h in range(H):
                c0 = 2 * h
                dst = do_s[:].rearrange("p (c t) -> p c t", c=NCH)[:, c0:c0 + 2,
                                                                  n * 128:(n + 1) * 128]
                em.copy('V', dst, Pps[h][:, 256:512].rearrange("p (c t) -> p c t", c=2),
                        reads=(PB[h],), writes=(DOS[c0], DOS[c0 + 1]))
            for h in range(H):
                for dkc in range(2):
                    em.mm(Sps[h][:, dkc * 256:(dkc + 1) * 256],
                          dKtm[h][:, dkc * 128:(dkc + 1) * 128], dU[h][:],
                          reads=(dKtm[h].name, dU[h].name), writes=(SB[h],),
                          start=(gn == 0 and dkc == 0), stop=(gn == L // C - 1 and dkc == 1))
        tap(f"do_t{tile}", do_s[:], DOS)

        if stage < 6:
            break

        for j in range(NJG):
            bi = j % 2
            pg, pgk = Pps[2], PB[2]
            pu, puk = Pps[3], PB[3]
            em.dma('S', wgs[bi][:], wg_b[:, j * 1024:(j + 1) * 1024],
                   reads=(f"wgs{bi}", "wg_b"), writes=(f"wgs{bi}",))
            em.dma('S', wus[bi][:], wu_b[:, j * 1024:(j + 1) * 1024],
                   reads=(f"wus{bi}", "wu_b"), writes=(f"wus{bi}",))
            em.dma('S', wds[bi][:], wd_b[:, j * 1024:(j + 1) * 1024],
                   reads=(f"wds{bi}", "wd_b"), writes=(f"wds{bi}",))
            for c in range(NCH):
                em.mm(pg[:, 0:T], wgs[bi][:, c * 128:(c + 1) * 128], xn[:, c * T:(c + 1) * T],
                      reads=(f"wgs{bi}", XN[c]), writes=(pgk,),
                      start=(c == 0), stop=(c == NCH - 1))
            for c in range(NCH):
                em.mm(pu[:, 0:T], wus[bi][:, c * 128:(c + 1) * 128], xn[:, c * T:(c + 1) * T],
                      reads=(f"wus{bi}", XN[c]), writes=(puk,),
                      start=(c == 0), stop=(c == NCH - 1))
            em.act(sg[:], pg[:, 0:T], AF.Silu, scale=1.0 / 127.0, reads=(pgk,), writes=("sg",))
            em.tt('V', hf[:], sg[:], pu[:, 0:T], OP.mult, reads=("sg", puk), writes=("hf",))
            for c in range(NCH):
                pd, pdk = (pg, pgk) if c % 2 == 0 else (pu, puk)
                em.mm(pd[:, 0:T], wds[bi][:, c * 128:(c + 1) * 128], hf[:],
                      reads=(f"wds{bi}", "hf"), writes=(pdk,), start=True, stop=True)
                em.stt('V', x_s[:, c * T:(c + 1) * T], pd[:, 0:T], FFN_SC,
                       x_s[:, c * T:(c + 1) * T], OP.mult, OP.add,
                       reads=(pdk, XS[c]), writes=(XS[c],))
        tap(f"xffn_t{tile}", x_s[:], XS)

        if stage < 7:
            break

        outsl = [(lo_s, LOS, False), (mi_s, MIS, False), (do_s, DOS, False), (v_s, VS, True)]
        for i_, (ss_, kk_, isv) in enumerate(outsl):
            for h in range(H):
                c0, c1 = 2 * h, 2 * h + 1
                a0 = vcol(c0, 0) if isv else c0 * T
                a1 = vcol(c1, 0) if isv else c1 * T
                for b in range(NBLK):
                    col = 10 + i_ * 4 + h
                    em.mm(Pps[b][:, col:col + 1], ss_[:, a0 + b * 128:a0 + (b + 1) * 128],
                          onesb[:, 0:1], reads=(kk_[c0], onesb.name), writes=(PB[b],),
                          start=True, stop=False)
                    em.mm(Pps[b][:, col:col + 1], ss_[:, a1 + b * 128:a1 + (b + 1) * 128],
                          onesb[:, 0:1], reads=(kk_[c1], onesb.name), writes=(PB[b],),
                          start=False, stop=True)
        pairs = [(0, 0), (1, 1), (2, 2), (3, 3), (0, 1), (0, 2), (0, 3), (1, 2), (1, 3), (2, 3)]
        for pi, (i_, j_) in enumerate(pairs):
            si, ki_, vi_ = outsl[i_]
            sjt, kj_, vj_ = outsl[j_]
            for h in range(H):
                for cc01 in range(2):
                    cc = 2 * h + cc01
                    ai = vcol(cc, 0) if vi_ else cc * T
                    aj = vcol(cc, 0) if vj_ else cc * T
                    dstsq = sq if cc01 == 0 else sq2
                    dk = "sq" if cc01 == 0 else "sq2"
                    if i_ == j_:
                        em.act(dstsq[:], si[:, ai:ai + T], AF.Square,
                               reads=(ki_[cc],), writes=(dk,))
                    else:
                        em.tt('V', dstsq[:], si[:, ai:ai + T], sjt[:, aj:aj + T], OP.mult,
                              reads=(ki_[cc], kj_[cc]), writes=(dk,))
                for b in range(NBLK):
                    col = 26 + pi * 4 + h
                    em.mm(Pps[b][:, col:col + 1], sq[:, b * 128:(b + 1) * 128], onesb[:, 0:1],
                          reads=("sq", onesb.name), writes=(PB[b],), start=True, stop=False)
                    em.mm(Pps[b][:, col:col + 1], sq2[:, b * 128:(b + 1) * 128], onesb[:, 0:1],
                          reads=("sq2", onesb.name), writes=(PB[b],), start=False, stop=True)
        em.fence('T', PB)
        for b in range(NBLK):
            mean_dst = ftmp[:, 0:32].rearrange("p (i g h) -> p i g h", g=2, h=4)[:, :, 0:1, :]
            var_dst = ftmp[:, 0:32].rearrange("p (i g h) -> p i g h", g=2, h=4)[:, :, 1:2, :]
            mean_src = Pps[b][:, 10:26].rearrange("p (i h) -> p i h", h=4)
            ss_src = Pps[b][:, 26:42].rearrange("p (i h) -> p i h", h=4)
            m3 = lambda t: t.rearrange("p (i h) -> p i h", h=4)
            em.ts('V', mean_dst, mean_src, 1.0 / DH, OP.mult, reads=(PB[b],), writes=("ftmp",))
            em.tt('V', m3(tokb[:, 0:16]), mean_dst, mean_dst, OP.mult,
                  reads=("ftmp",), writes=("tokb",))
            em.stt('V', m3(toke[:, 0:16]), ss_src, 1.0 / (DH - 1), m3(tokb[:, 0:16]),
                   OP.mult, OP.subtract, reads=(PB[b], "tokb"), writes=("toke",))
            em.stt('V', var_dst, m3(tokb[:, 0:16]), -1.0 / (DH - 1), m3(toke[:, 0:16]),
                   OP.mult, OP.add, reads=("tokb", "toke"), writes=("ftmp",))
            em.ts('V', ftmp[:, 32:56], Pps[b][:, 42:66], 1.0 / DH, OP.mult,
                  reads=(PB[b],), writes=("ftmp",))
            em.trans(Pps[b][0:56, 128:256], ftmp[:, 0:56], idf[:],
                     reads=("ftmp", idf.name), writes=(PB[b],))
            em.fence('T', (PB[b],))
            em.copy('V', feats[0:56, b * 128:(b + 1) * 128], Pps[b][0:56, 128:256],
                    reads=(PB[b],), writes=("feats",))
        tap(f"feats_t{tile}", feats[:], ("feats",))

        if stage < 8:
            break

        for nsl in range(17):
            bi = nsl % 2
            ph, phk = (Pps[0], PB[0]) if bi == 0 else (Pps[2], PB[2])
            em.dma('S', rw1s[bi][:], rw1_b[:, nsl * RK * 128:(nsl + 1) * RK * 128],
                   reads=(f"rw1s{bi}", "rw1_b"), writes=(f"rw1s{bi}",))
            for kc in range(RK):
                rhs = xn[:, kc * T:(kc + 1) * T] if kc < 8 else feats[0:128, 0:T]
                rk = XN[kc] if kc < 8 else "feats"
                em.mm(ph[:, 0:T], rw1s[bi][:, kc * 128:(kc + 1) * 128], rhs,
                      reads=(f"rw1s{bi}", rk), writes=(phk,),
                      start=(kc == 0), stop=(kc == RK - 1))
            h1buf = h1a if bi == 0 else h1b
            em.act(h1buf[:], ph[:, 0:T], AF.Silu, bias=rb1[:, nsl:nsl + 1],
                   reads=(phk, rb1.name), writes=(h1buf.name,))
            em.mm(Pps[1][0:16, 0:T], rw2[:, nsl * 16:(nsl + 1) * 16], h1buf[:],
                  reads=(rw2.name, h1buf.name), writes=(PB[1],),
                  start=(nsl == 0), stop=(nsl == 16))
        em.ts('V', lgt[0:16, :], Pps[1][0:16, 0:T], rb2[0:16, 0:1], OP.add,
              reads=(PB[1], rb2.name), writes=("lgt",))
        for b in range(NBLK):
            em.trans(Pps[3][0:128, b * 16:(b + 1) * 16], lgt[0:16, b * 128:(b + 1) * 128],
                     idf[0:16, 0:16], reads=("lgt", idf.name), writes=(PB[3],))
        em.fence('T', (PB[3],))
        em.copy('V', tokb[:, 0:64], Pps[3][:, 0:64], reads=(PB[3],), writes=("tokb",))
        em.op('V', 'tensor_reduce', out=tok[:, 0:16],
              in_=tokb[:, 0:64].rearrange("p (g i) -> p g i", i=4),
              axis=AX.X, op=OP.max, reads=("tokb",), writes=("tok",))
        for i_ in range(4):
            em.tt('V', toke[:, i_:64:4], tokb[:, i_:64:4], tok[:, 0:16], OP.subtract,
                  reads=("tokb", "tok"), writes=("toke",))
        em.act(toke[:, 0:64], toke[:, 0:64], AF.Tanh, scale=0.5, reads=("toke",),
               writes=("toke",))
        em.ts('V', tokb[:, 0:64], toke[:, 0:64], 1.0, OP.add, reads=("toke",), writes=("tokb",))
        em.ts('V', toke[:, 0:64], toke[:, 0:64], -1.0, OP.mult, 1.0, OP.add,
              reads=("toke",), writes=("toke",))
        em.op('V', 'reciprocal', toke[:, 0:64], toke[:, 0:64], reads=("toke",), writes=("toke",))
        em.tt('V', toke[:, 0:64], tokb[:, 0:64], toke[:, 0:64], OP.mult,
              reads=("tokb", "toke"), writes=("toke",))
        em.op('V', 'tensor_reduce', out=tok[:, 0:16],
              in_=toke[:, 0:64].rearrange("p (g i) -> p g i", i=4),
              axis=AX.X, op=OP.add, reads=("toke",), writes=("tok",))
        em.op('V', 'reciprocal', tok[:, 0:16], tok[:, 0:16], reads=("tok",), writes=("tok",))
        for i_ in range(4):
            em.tt('V', toke[:, i_:64:4], toke[:, i_:64:4], tok[:, 0:16], OP.mult,
                  reads=("toke", "tok"), writes=("toke",))
        em.ts('V', toke[:, 0:64], toke[:, 0:64], 0.96, OP.mult, 0.01, OP.add,
              reads=("toke",), writes=("toke",))
        tap(f"p_t{tile}", toke[:, 0:64], ("toke",))

        if stage < 9:
            break

        for h in range(H):
            cc0, cc1 = 2 * h, 2 * h + 1
            for i_, (ss_, kk_, isv) in enumerate(outsl):
                a0 = vcol(cc0, 0) if isv else cc0 * T
                a1 = vcol(cc1, 0) if isv else cc1 * T
                pbc, pbck = (Pps[2], PB[2]) if i_ % 2 == 0 else (Pps[3], PB[3])
                for b in range(NBLK):
                    col = b * 16 + h * 4 + i_
                    em.ts('V', diag[:], idb[:], toke[:, col:col + 1], OP.mult,
                          reads=(idb.name, "toke"), writes=("diag",))
                    em.mm(pbc[:, b * 128:(b + 1) * 128], onesb[:], diag[:],
                          reads=("diag", onesb.name), writes=(pbck,), start=True, stop=True)
                if i_ == 0:
                    em.tt('V', mxt[:], ss_[:, a0:a0 + T], pbc[:, 0:T], OP.mult,
                          reads=(kk_[cc0], pbck), writes=("mxt",))
                    em.tt('V', sg[:], ss_[:, a1:a1 + T], pbc[:, 0:T], OP.mult,
                          reads=(kk_[cc1], pbck), writes=("sg",))
                else:
                    em.tt('V', hf[:], ss_[:, a0:a0 + T], pbc[:, 0:T], OP.mult,
                          reads=(kk_[cc0], pbck), writes=("hf",))
                    em.tt('V', mxt[:], mxt[:], hf[:], OP.add,
                          reads=("mxt", "hf"), writes=("mxt",))
                    em.tt('V', hf[:], ss_[:, a1:a1 + T], pbc[:, 0:T], OP.mult,
                          reads=(kk_[cc1], pbck), writes=("hf",))
                    em.tt('V', sg[:], sg[:], hf[:], OP.add,
                          reads=("sg", "hf"), writes=("sg",))
            em.copy('V', lo_s[:, cc0 * T:(cc0 + 1) * T], mxt[:], reads=("mxt",),
                    writes=(LOS[cc0],))
            em.copy('V', lo_s[:, cc1 * T:(cc1 + 1) * T], sg[:], reads=("sg",),
                    writes=(LOS[cc1],))
            em.act(sq[:], mxt[:], AF.Square, reads=("mxt",), writes=("sq",))
            em.act(sq2[:], sg[:], AF.Square, reads=("sg",), writes=("sq2",))
            for b in range(NBLK):
                em.mm(Pps[b][:, 0:1], sq[:, b * 128:(b + 1) * 128], onesb[:, 0:1],
                      reads=("sq", onesb.name), writes=(PB[b],), start=True, stop=False)
                em.mm(Pps[b][:, 0:1], sq2[:, b * 128:(b + 1) * 128], onesb[:, 0:1],
                      reads=("sq2", onesb.name), writes=(PB[b],), start=False, stop=True)
            em.ts('V', sq[:], sq[:], mw2[:, 2 * h:2 * h + 1], OP.mult,
                  reads=("sq", mw2.name), writes=("sq",))
            em.ts('V', sq2[:], sq2[:], mw2[:, 2 * h + 1:2 * h + 2], OP.mult,
                  reads=("sq2", mw2.name), writes=("sq2",))
            for b in range(NBLK):
                em.mm(Pps[b][:, 1:2], sq[:, b * 128:(b + 1) * 128], onesb[:, 0:1],
                      reads=("sq", onesb.name), writes=(PB[b],), start=True, stop=False)
                em.mm(Pps[b][:, 1:2], sq2[:, b * 128:(b + 1) * 128], onesb[:, 0:1],
                      reads=("sq2", onesb.name), writes=(PB[b],), start=False, stop=True)
            em.fence('T', PB)
            for b in range(NBLK):
                em.copy('V', tok[:, 16 + b * 2:16 + b * 2 + 2], Pps[b][:, 0:2],
                        reads=(PB[b],), writes=("tok",))
            em.stt('V', tok[:, 24:28], tok[:, 16:24:2], 1e-5, tok[:, 17:24:2],
                   OP.mult, OP.add, reads=("tok",), writes=("tok",))
            em.ts('V', tok[:, 24:28], tok[:, 24:28], 1.0 / DH, OP.mult, 1e-10, OP.add,
                  reads=("tok",), writes=("tok",))
            em.act(tok[:, 24:28], tok[:, 24:28], AF.Sqrt, reads=("tok",), writes=("tok",))
            em.op('V', 'reciprocal', tok[:, 24:28], tok[:, 24:28], reads=("tok",), writes=("tok",))
            for b in range(NBLK):
                em.ts('V', diag[:], idb[:], tok[:, 24 + b:25 + b], OP.mult,
                      reads=(idb.name, "tok"), writes=("diag",))
                em.mm(Pps[3][:, b * 128:(b + 1) * 128], onesb[:], diag[:],
                      reads=("diag", onesb.name), writes=(PB[3],), start=True, stop=True)
            for cc01 in range(2):
                cc = 2 * h + cc01
                em.stt('V', k_s[:, cc * T:(cc + 1) * T], lo_s[:, cc * T:(cc + 1) * T],
                       w2[:, cc:cc + 1], Pps[3][:, 0:T], OP.mult, OP.mult,
                       reads=(LOS[cc], w2.name, PB[3]), writes=(KS[cc],))
        tap(f"on_t{tile}", k_s[:], KS)

        if stage < 10:
            break

        for nn in range(NCH):
            pb, pk = (Pps[0], PB[0]) if nn % 2 == 0 else (Pps[1], PB[1])
            for c in range(NCH):
                em.mm(pb[:, 0:T], wo[:, c * D + nn * 128:c * D + (nn + 1) * 128],
                      k_s[:, c * T:(c + 1) * T], reads=(wo.name, KS[c]), writes=(pk,),
                      start=(c == 0), stop=(c == NCH - 1))
            em.tt('V', x_s[:, nn * T:(nn + 1) * T], x_s[:, nn * T:(nn + 1) * T], pb[:, 0:T],
                  OP.add, reads=(XS[nn], pk), writes=(XS[nn],))
        dsto = out_h.rearrange("(c p) L -> p c L", p=128)[:, :, t0:t0 + T]
        em.dma('S', dsto, x_s[:].rearrange("p (c t) -> p c t", c=NCH),
               reads=tuple(XS), writes=())

    for q in 'SP':
        for i in range(em.NDL):
            if em.cntdl[(q, i)]:
                em.e[q].wait_ge(em.semd[q][i], 16 * em.cntdl[(q, i)])
    return nc, dbg


_PROG_CACHE = {}


def _host_inputs(inputs):
    f32 = np.float32

    def chunkw(w, nk, m):
        return np.ascontiguousarray(w.reshape(nk, 128, m).transpose(1, 0, 2).reshape(128, nk * m))

    rw1 = np.asarray(inputs["L0_attn_r_w1"], f32)
    rb1 = np.asarray(inputs["L0_attn_r_b1"], f32)
    rw2 = np.asarray(inputs["L0_attn_r_w2"], f32)
    rb2 = np.asarray(inputs["L0_attn_r_b2"], f32)
    rw1p = np.zeros((RK * 128, RH), f32)
    rw1p[:1080, :2160] = rw1
    rb1p = np.zeros((RH,), f32)
    rb1p[:2160] = rb1
    rw2p = np.zeros((RH, 16), f32)
    rw2p[:2160] = rw2
    rb2p = np.zeros((128, 1), f32)
    rb2p[:16, 0] = rb2

    def conv(nm, K):
        return chunkw(np.asarray(inputs[nm], f32), 8, K)

    wgT = np.ascontiguousarray(np.asarray(inputs["L0_ffn_w_gate"], np.int32).T)
    wuT = np.ascontiguousarray(np.asarray(inputs["L0_ffn_w_up"], np.int32).T)
    wdT = np.ascontiguousarray(np.asarray(inputs["L0_ffn_w_down"], np.int32).T)
    mwr = np.asarray(inputs["L0_attn_mix_w"], f32).reshape(4, 2, 128)
    mw_sb = np.ascontiguousarray(mwr.transpose(2, 0, 1).reshape(128, 8))
    onr = np.asarray(inputs["L0_attn_onorm_w"], f32).reshape(2, 128).T
    onorm_sb = np.ascontiguousarray(np.tile(onr, (1, 4)))
    return {
        "wq": chunkw(np.asarray(inputs["L0_attn_Wq"], f32), 8, D),
        "wk": chunkw(np.asarray(inputs["L0_attn_Wk"], f32), 8, D),
        "wv": chunkw(np.asarray(inputs["L0_attn_Wv"], f32), 8, D),
        "wb": chunkw(np.asarray(inputs["L0_attn_Wb"], f32), 8, 4),
        "wo": chunkw(np.asarray(inputs["L0_attn_Wo"], f32), 8, D),
        "rw1": chunkw(rw1p, RK, RH),
        "rb1": np.ascontiguousarray(rb1p.reshape(17, 128).T),
        "rw2": chunkw(rw2p, 17, 16), "rb2": rb2p,
        "cq": conv("L0_attn_conv_q_w", 4), "ck": conv("L0_attn_conv_k_w", 4),
        "cv": conv("L0_attn_conv_v_w", 4),
        "cl": conv("L0_attn_local_w", 7), "cm": conv("L0_attn_mid_w", 25),
        "mw": mw_sb, "onorm": onorm_sb,
        "wg": chunkw(wgT, 8, HID), "wu": chunkw(wuT, 8, HID),
        "wd": chunkw(wdT, NJG, 1024),
        "identbf": np.eye(128, dtype=BF).view(np.uint16),
        "identf32": np.eye(128, dtype=f32),
        "onesbf": np.ones((128, 128), BF).view(np.uint16),
        "onesf32": np.ones((128, 1), f32),
        "masklow": np.tril(np.ones((128, 128), f32), -1).astype(BF).view(np.uint16),
        "maskupincl": np.triu(np.ones((128, 128), f32), 0).astype(BF).view(np.uint16),
    }


def kernel(**inputs):
    global DEBUG_RESULTS
    taps = tuple(DEBUG_TAPS)
    key = (taps, STAGE, QKV_TENSORS, QKV_NX, QKV_CONV)
    if key not in _PROG_CACHE:
        _PROG_CACHE[key] = build_program(taps, STAGE)
    nc, dbg = _PROG_CACHE[key]
    shared = _host_inputs(inputs)
    x = np.asarray(inputs["L0_attn_hidden_states"], np.float32)
    in_maps = []
    for core in range(8):
        m = dict(shared)
        m["x"] = np.ascontiguousarray(x[core].T)
        in_maps.append(m)
    res = run_bass_kernel_spmd(nc, in_maps, core_ids=list(range(8)))
    out = np.stack([np.ascontiguousarray(np.asarray(r["out"]).T) for r in res.results])
    if taps:
        def cv(a, isbf):
            a = np.asarray(a)
            if isbf and a.dtype == np.uint16:
                return a.view(BF).astype(np.float32)
            if a.dtype == BF:
                return a.astype(np.float32)
            return a
        DEBUG_RESULTS = [{k: cv(r["dbg_" + k], isbf) for k, isbf in dbg.items()}
                         for r in res.results]
    return out
